# version 01
# baseline (speedup 1.0000x reference)
import math
import numpy as np
import ml_dtypes
from contextlib import ExitStack

import concourse.bass as bass
import concourse.mybir as mybir
from concourse.bass_utils import run_bass_kernel_spmd

F32 = mybir.dt.float32
BF16 = mybir.dt.bfloat16
I32 = mybir.dt.int32
AF = mybir.ActivationFunctionType
ALU = mybir.AluOpType
AX = mybir.AxisListType

DM = 4096
SEQ = 4096
NTOK = 1024
NT = 8
NC = 32
EPS_LN = 1e-5
EPS_RMS = 1e-6
BF = ml_dtypes.bfloat16
N_CORES = 8


def core_tiles(c):
    s, j = divmod(c, 4)
    return s, [j, j + 4, j + 8, j + 12, 19 - j, 23 - j, 27 - j, 31 - j]



QUEUES = ("sync", "gpsimd")
ENGINES = ("tensor", "scalar", "vector", "gpsimd", "sync")


class Res:
    __slots__ = ("name",)

    def __init__(self, name=""):
        self.name = name

    def __repr__(self):
        return f"R({self.name})"


class Builder:
    def __init__(self, nc, stack):
        self.nc = nc
        self.sems = {}
        for e in ENGINES:
            self.sems[e] = stack.enter_context(nc.semaphore(f"sem_{e}"))
        self.sems["cc"] = stack.enter_context(nc.semaphore("sem_cc"))
        self.count = {s: 0 for s in self.sems}
        self.ops = {e: [] for e in ENGINES}
        self.last_w = {}
        self.readers = {}
        self.waited = {e: {s: 0 for s in self.sems} for e in ENGINES}

    def _dep(self, sem_name, cnt):
        if sem_name in QUEUES:
            return self.count[sem_name]
        return cnt

    def op(self, engine, emit, reads=(), writes=(), inc=1):
        deps = {}
        for r in reads:
            lw = self.last_w.get(r)
            if lw is not None:
                deps[lw[0]] = max(deps.get(lw[0], 0), self._dep(*lw))
        for w in writes:
            lw = self.last_w.get(w)
            if lw is not None:
                deps[lw[0]] = max(deps.get(lw[0], 0), self._dep(*lw))
            for rr in self.readers.get(w, ()):
                deps[rr[0]] = max(deps.get(rr[0], 0), self._dep(*rr))
        waits = []
        wcl = self.waited[engine]
        for s, c in sorted(deps.items()):
            if c <= wcl.get(s, 0):
                continue
            if s == engine and engine == "tensor":
                continue
            waits.append((s, c))
            wcl[s] = c
        self.ops[engine].append((tuple(waits), emit, inc))
        self.count[engine] += inc
        sig = (engine, self.count[engine])
        for w in writes:
            self.last_w[w] = sig
            self.readers[w] = []
        for r in reads:
            self.readers.setdefault(r, []).append(sig)
        return sig

    def dma(self, queue, emit, reads=(), writes=()):
        return self.op(queue, emit, reads=reads, writes=writes, inc=16)

    def collective(self, emit, reads=(), writes=()):
        cc = self.sems["cc"]
        self.count["cc"] += 1
        k = self.count["cc"]

        def emit2(e):
            emit(e).then_inc(cc)
            return None

        self.op("gpsimd", emit2, reads=reads, writes=(), inc=0)
        for w in writes:
            self.last_w[w] = ("cc", k)
            self.readers[w] = []

    def final_waits(self):
        for s in self.sems:
            if s == "sync":
                continue
            c = self.count[s]
            if c > self.waited["sync"].get(s, 0):
                self.ops["sync"].append((((s, c),), None, 0))
                self.waited["sync"][s] = c
        self.ops["sync"].append(((("sync", self.count["sync"]),), None, 0))

    def finalize(self):
        nc = self.nc
        sems = self.sems
        self.final_waits()
        with nc.Block() as block:
            def make_body(name):
                ops = self.ops[name]
                sem = sems[name]

                def body(eng):
                    for waits, emit, inc in ops:
                        for (s, c) in waits:
                            eng.wait_ge(sems[s], c)
                        if emit is None:
                            continue
                        inst = emit(eng)
                        if inst is not None and inc > 0:
                            inst.then_inc(sem, inc)
                return body

            block.sync(make_body("sync"))
            block.scalar(make_body("scalar"))
            block.vector(make_body("vector"))
            block.tensor(make_body("tensor"))
            block.gpsimd(make_body("gpsimd"))


class T:
    __slots__ = ("h", "r")

    def __init__(self, h, name=""):
        self.h = h
        self.r = Res(name)


def sb(nc, name, shape, dtype):
    return T(nc.alloc_sbuf_tensor(name, list(shape), dtype), name)


def psum(nc, name, shape, dtype=F32):
    return T(nc.alloc_psum_tensor(name, list(shape), dtype), name)


class DT:
    __slots__ = ("h", "r", "a")

    def __init__(self, h, name=""):
        self.h = h
        self.r = Res(name)
        self.a = h.ap()


def dram(nc, name, shape, dtype, kind="Internal"):
    return DT(nc.dram_tensor(name, list(shape), dtype, kind=kind), name)



class Ctx:
    def __init__(self, nc, b):
        self.nc = nc
        self.b = b
        self.ident = sb(nc, "sb_ident", [128, 128], BF16)
        self.onec_bf = sb(nc, "sb_onec", [128, 1], BF16)
        self.oner_f = sb(nc, "sb_onerf", [1, 128], F32)
        self.oner_bf = sb(nc, "sb_onerbf", [1, 128], BF16)
        self.pb = [psum(nc, f"pbank{i}", [128, 512]) for i in range(8)]
        self.row_a = sb(nc, "row_a", [1, 1024], F32)
        self.row_b = sb(nc, "row_b", [1, 1024], F32)
        self.row_c = sb(nc, "row_c", [1, 1024], F32)
        self.m_b = sb(nc, "m_b", [128, 1024], F32)
        self.r_b = sb(nc, "r_b", [128, 1024], F32)
        self.xa = [sb(nc, f"xa{i}", [128, 1024], F32) for i in range(2)]
        self.ya = [sb(nc, f"ya{i}", [128, 1024], F32) for i in range(2)]
        self.sbf = [sb(nc, f"sbf{i}", [128, 1024], BF16) for i in range(2)]
        self.xh = [sb(nc, f"xh{i}", [128, 1024], BF16) for i in range(NC)]
        self._csts = {}

    def cst(self, val):
        if val not in self._csts:
            t = sb(self.nc, f"cst_{len(self._csts)}", [128, 1], F32)
            self.b.op("vector", lambda e, t=t, val=val: e.memset(t.h[:, :], float(val)),
                      reads=(), writes=(t.r,))
            self._csts[val] = t
        t = self._csts[val]
        return t.h[:, 0:1], t.r

    def load_consts(self, ident_d, ones_d):
        b = self.b
        b.dma("sync", lambda e: e.dma_start(out=self.ident.h[:, :], in_=ident_d.a[:, :]),
              reads=(ident_d.r,), writes=(self.ident.r,))
        b.dma("gpsimd", lambda e: e.dma_start(out=self.onec_bf.h[:, :], in_=ones_d.a[0:128, 0:1],
                                              allow_slow_non_contiguous=True),
              reads=(ones_d.r,), writes=(self.onec_bf.r,))
        b.dma("sync", lambda e: e.dma_start(out=self.oner_f.h[:, :], in_=ones_d.a[0:1, 0:128]),
              reads=(ones_d.r,), writes=(self.oner_f.r,))
        b.op("vector", lambda e: e.memset(self.oner_bf.h[:, :], 1.0),
             reads=(), writes=(self.oner_bf.r,))

    def transpose_tile(self, src_ap, src_res, dst_tile, dst_p0, dst_col, pbank,
                       np_part=128, n_cols=128):
        b = self.b
        pview = self.pb[pbank].h[:, :].bitcast(BF16)
        out_ps = pview[0:n_cols, 0:np_part]
        ident = self.ident.h[0:np_part, 0:np_part]
        b.op("tensor", lambda e: e.transpose(out_ps, src_ap, ident),
             reads=(src_res, self.ident.r), writes=(self.pb[pbank].r,))
        b.op("vector", lambda e: e.tensor_copy(
            dst_tile.h[dst_p0:dst_p0 + n_cols, dst_col:dst_col + np_part], out_ps),
            reads=(self.pb[pbank].r,), writes=(dst_tile.r,))

    def bcast_row(self, row_tile, out_tile, pbank=0, n=1024, row_ap=None):
        b = self.b
        for j in range(0, n, 512):
            w = min(512, n - j)
            pbv = self.pb[pbank]
            rap = row_ap[0:1, j:j + w] if row_ap is not None else row_tile.h[0:1, j:j + w]
            b.op("tensor", lambda e, j=j, w=w, rap=rap: e.matmul(
                pbv.h[:, 0:w], self.oner_f.h[0:1, :], rap,
                start=True, stop=True),
                reads=(row_tile.r, self.oner_f.r), writes=(pbv.r,))
            b.op("vector", lambda e, j=j, w=w: e.tensor_copy(
                out_tile.h[:, j:j + w], pbv.h[:, 0:w]),
                reads=(pbv.r,), writes=(out_tile.r,))


def st_sum_view(ctx, j):
    t = ctx.pb[6 + j // 512]
    return t, t.h[0:1, :]


def st_sq_view(ctx, j):
    t = ctx.pb[4 + j // 512]
    return t, t.h[0:1, :]


def chunk_stats(ctx, src_tile, first, last, n=1024):
    b = ctx.b
    cast, sq = ctx.sbf[0], ctx.sbf[1]
    b.op("vector", lambda e: e.tensor_copy(cast.h[:, 0:n], src_tile.h[:, 0:n]),
         reads=(src_tile.r,), writes=(cast.r,))
    b.op("scalar", lambda e: e.activation(sq.h[:, 0:n], src_tile.h[:, 0:n], AF.Square),
         reads=(src_tile.r,), writes=(sq.r,))
    for j in range(0, n, 512):
        ts, vs = st_sum_view(ctx, j)
        tq, vq = st_sq_view(ctx, j)
        b.op("tensor", lambda e, j=j, vs=vs: e.matmul(
            vs, ctx.onec_bf.h[:, 0:1], cast.h[:, j:j + 512],
            start=first, stop=last),
            reads=(cast.r, ctx.onec_bf.r), writes=(ts.r,))
        b.op("tensor", lambda e, j=j, vq=vq: e.matmul(
            vq, ctx.onec_bf.h[:, 0:1], sq.h[:, j:j + 512],
            start=first, stop=last),
            reads=(sq.r, ctx.onec_bf.r), writes=(tq.r,))


def finalize_ln_stats(ctx, n_feat, eps):
    b = ctx.b
    inv_n = 1.0 / n_feat
    for j in (0, 512):
        ts, vs = st_sum_view(ctx, j)
        tq, vq = st_sq_view(ctx, j)
        b.op("vector", lambda e, j=j, vs=vs: e.tensor_single_scalar(
            ctx.row_a.h[0:1, j:j + 512], vs, inv_n, ALU.mult),
            reads=(ts.r,), writes=(ctx.row_a.r,))
        b.op("vector", lambda e, j=j, vq=vq: e.tensor_single_scalar(
            ctx.row_b.h[0:1, j:j + 512], vq, inv_n, ALU.mult),
            reads=(tq.r,), writes=(ctx.row_b.r,))
    b.op("scalar", lambda e: e.activation(ctx.row_c.h[:, :], ctx.row_a.h[:, :], AF.Square),
         reads=(ctx.row_a.r,), writes=(ctx.row_c.r,))
    b.op("vector", lambda e: e.tensor_sub(ctx.row_b.h[:, :], ctx.row_b.h[:, :], ctx.row_c.h[:, :]),
         reads=(ctx.row_b.r, ctx.row_c.r), writes=(ctx.row_b.r,))
    eps_ap, eps_r = ctx.cst(float(eps))
    b.op("scalar", lambda e: e.activation(ctx.row_b.h[0:1, :], ctx.row_b.h[0:1, :], AF.Sqrt,
                                          bias=eps_ap[0:1, 0:1]),
         reads=(ctx.row_b.r, eps_r), writes=(ctx.row_b.r,))
    b.op("vector", lambda e: e.reciprocal(ctx.row_b.h[:, :], ctx.row_b.h[:, :]),
         reads=(ctx.row_b.r,), writes=(ctx.row_b.r,))
    ctx.bcast_row(ctx.row_a, ctx.m_b, pbank=0)
    ctx.bcast_row(ctx.row_b, ctx.r_b, pbank=1)


def ln_stats_of_dram(ctx, x_d):
    b = ctx.b
    for c in range(NC):
        xa = ctx.xa[c % 2]
        b.dma("sync", lambda e, c=c, xa=xa: e.dma_start(
            out=xa.h[:, :], in_=x_d.a[c * 128:(c + 1) * 128, :]),
            reads=(x_d.r,), writes=(xa.r,))
        chunk_stats(ctx, xa, first=(c == 0), last=(c == NC - 1))
    finalize_ln_stats(ctx, DM, EPS_LN)


def xhat_pass(ctx, x_d):
    b = ctx.b
    for c in range(NC):
        xa = ctx.xa[c % 2]
        b.dma("sync", lambda e, c=c, xa=xa: e.dma_start(
            out=xa.h[:, :], in_=x_d.a[c * 128:(c + 1) * 128, :]),
            reads=(x_d.r,), writes=(xa.r,))
        b.op("vector", lambda e, xa=xa: e.tensor_sub(xa.h[:, :], xa.h[:, :], ctx.m_b.h[:, :]),
             reads=(xa.r, ctx.m_b.r), writes=(xa.r,))
        b.op("vector", lambda e, c=c, xa=xa: e.tensor_mul(
            ctx.xh[c].h[:, :], xa.h[:, :], ctx.r_b.h[:, :]),
            reads=(xa.r, ctx.r_b.r), writes=(ctx.xh[c].r,))


def outer_ln_residual(ctx, y_d, xsrc_d, xdst_d, collect_next_stats=False):
    b = ctx.b
    for c in range(NC):
        xa = ctx.xa[c % 2]
        ya = ctx.ya[c % 2]
        b.dma("sync", lambda e, c=c, xa=xa: e.dma_start(
            out=xa.h[:, :], in_=xsrc_d.a[c * 128:(c + 1) * 128, :]),
            reads=(xsrc_d.r,), writes=(xa.r,))
        b.dma("sync", lambda e, c=c, ya=ya: e.dma_start(
            out=ya.h[:, :], in_=y_d.a[c * 128:(c + 1) * 128, :]),
            reads=(y_d.r,), writes=(ya.r,))
        b.op("vector", lambda e, ya=ya: e.tensor_sub(ya.h[:, :], ya.h[:, :], ctx.m_b.h[:, :]),
             reads=(ya.r, ctx.m_b.r), writes=(ya.r,))
        b.op("vector", lambda e, ya=ya: e.tensor_mul(ya.h[:, :], ya.h[:, :], ctx.r_b.h[:, :]),
             reads=(ya.r, ctx.r_b.r), writes=(ya.r,))
        b.op("vector", lambda e, xa=xa, ya=ya: e.tensor_add(xa.h[:, :], xa.h[:, :], ya.h[:, :]),
             reads=(xa.r, ya.r), writes=(xa.r,))
        if collect_next_stats:
            chunk_stats(ctx, xa, first=(c == 0), last=(c == NC - 1))
        b.dma("sync", lambda e, c=c, xa=xa: e.dma_start(
            out=xdst_d.a[c * 128:(c + 1) * 128, :], in_=xa.h[:, :]),
            reads=(xa.r,), writes=(xdst_d.r,))
    if collect_next_stats:
        finalize_ln_stats(ctx, DM, EPS_LN)


def evac_y_chunk(ctx, y_sb, oc, y_d, first, last):
    b = ctx.b
    for th in range(2):
        pbv = ctx.pb[th]
        b.op("vector", lambda e, th=th, pbv=pbv: e.tensor_copy(
            y_sb.h[:, th * 512:(th + 1) * 512], pbv.h[:, :]),
            reads=(pbv.r,), writes=(y_sb.r,))
    chunk_stats(ctx, y_sb, first=first, last=last)
    b.dma("sync", lambda e, oc=oc: e.dma_start(
        out=y_d.a[oc * 128:(oc + 1) * 128, :], in_=y_sb.h[:, :]),
        reads=(y_sb.r,), writes=(y_d.r,))



def emit_l1_ffn(ctx, io):
    nc, b = ctx.nc, ctx.b

    pk = sb(nc, "f_pk", [128, 2048], I32)
    pki = sb(nc, "f_pki", [128, 1024], I32)
    dq_w = sb(nc, "f_w", [128, 2048], F32)
    dq_s = sb(nc, "f_s", [128, 2048], F32)
    dq_m = sb(nc, "f_m", [128, 2048], F32)
    dq_r = sb(nc, "f_r", [128, 2048], F32)
    bsc = sb(nc, "f_bs", [128, 2048], F32)
    g128 = sb(nc, "f_g", [128, 1], F32)
    wd = sb(nc, "f_wd", [128, 4096], BF16)
    wt = sb(nc, "f_wt", [128, 4096], BF16)
    at_t = sb(nc, "f_at", [128, 512], BF16)
    bt_t = sb(nc, "f_bt", [16, 4096], BF16)
    a_nat = sb(nc, "f_an", [16, 4096], BF16)
    b_nat = sb(nc, "f_bn", [128, 16], BF16)
    tr_f = sb(nc, "f_tr", [16, 1024], BF16)
    y_sb = sb(nc, "f_y", [128, 1024], F32)

    packed, bs16, g_d = io["packed"], io["bs16"], io["g128"]
    lora_A, lora_B, y_d = io["lora_A"], io["lora_B"], io["y_d"]

    b.dma("sync", lambda e: e.dma_start(out=g128.h[:, :], in_=g_d.a[:, :]),
          reads=(g_d.r,), writes=(g128.r,))

    b.dma("gpsimd", lambda e: e.dma_start(out=a_nat.h[:, :], in_=lora_A.a[:, :]),
          reads=(lora_A.r,), writes=(a_nat.r,))
    for c in range(NC):
        ctx.transpose_tile(a_nat.h[0:16, c * 128:(c + 1) * 128], a_nat.r,
                           at_t, 0, c * 16, pbank=2 + (c % 2), np_part=16, n_cols=128)
    for c in range(NC):
        b.dma("gpsimd", lambda e, c=c: e.dma_start(
            out=b_nat.h[:, :], in_=lora_B.a[c * 128:(c + 1) * 128, :]),
            reads=(lora_B.r,), writes=(b_nat.r,))
        ctx.transpose_tile(b_nat.h[:, 0:16], b_nat.r, bt_t, 0, c * 128,
                           pbank=2 + (c % 2), np_part=128, n_cols=16)
    for th in range(2):
        pairs = [(at_t.h[:, c * 16:(c + 1) * 16], ctx.xh[c].h[:, th * 512:(th + 1) * 512])
                 for c in range(NC)]

        def emit_mm(e, pairs=pairs):
            inst = None
            n = len(pairs)
            for i, (lh, rh) in enumerate(pairs):
                inst = e.matmul(ctx.pb[2].h[0:16, :], lh, rh, start=(i == 0), stop=(i == n - 1))
            return inst
        b.op("tensor", emit_mm,
             reads=tuple([at_t.r] + [ctx.xh[c].r for c in range(NC)]),
             writes=(ctx.pb[2].r,))
        b.op("vector", lambda e, th=th: e.tensor_copy(
            tr_f.h[:, th * 512:(th + 1) * 512], ctx.pb[2].h[0:16, :]),
            reads=(ctx.pb[2].r,), writes=(tr_f.r,))

    for oc in range(NC):
        b.dma("sync", lambda e, oc=oc: e.dma_start(
            out=pk.h[:, :], in_=packed.a[oc * 128:(oc + 1) * 128, :]),
            reads=(packed.r,), writes=(pk.r,))
        for hh in range(2):
            b.dma("sync", lambda e, oc=oc, hh=hh: e.dma_start(
                out=bsc.h[:, :], in_=bs16.a[oc * 128:(oc + 1) * 128, hh * 2048:(hh + 1) * 2048]),
                reads=(bs16.r,), writes=(bsc.r,))
            pkh = pk.h[:, hh * 1024:(hh + 1) * 1024]
            b.op("vector", lambda e, pkh=pkh: e.tensor_single_scalar(
                pki.h[:, :], pkh, 15, ALU.bitwise_and),
                reads=(pk.r,), writes=(pki.r,))
            b.op("vector", lambda e: e.tensor_copy(dq_w.h[:, 0:2048:2], pki.h[:, :]),
                 reads=(pki.r,), writes=(dq_w.r,))
            b.op("vector", lambda e, pkh=pkh: e.tensor_scalar(
                pki.h[:, :], pkh, 4, 15, ALU.logical_shift_right, ALU.bitwise_and),
                reads=(pk.r,), writes=(pki.r,))
            b.op("vector", lambda e: e.tensor_copy(dq_w.h[:, 1:2048:2], pki.h[:, :]),
                 reads=(pki.r,), writes=(dq_w.r,))
            b.op("gpsimd", lambda e: e.tensor_single_scalar(
                dq_s.h[:, :], dq_w.h[:, :], 8.0, ALU.is_ge),
                reads=(dq_w.r,), writes=(dq_s.r,))
            b.op("vector", lambda e: e.scalar_tensor_tensor(
                dq_m.h[:, :], dq_s.h[:, :], -8.0, dq_w.h[:, :], ALU.mult, ALU.add),
                reads=(dq_s.r, dq_w.r), writes=(dq_m.r,))
            bm2, bm2r = ctx.cst(-2.0)
            bm6, bm6r = ctx.cst(-6.0)
            b.op("scalar", lambda e: e.activation(
                dq_r.h[:, :], dq_m.h[:, :], AF.Relu, bias=bm2, scale=0.5),
                reads=(dq_m.r, bm2r), writes=(dq_r.r,))
            b.op("scalar", lambda e: e.activation(
                dq_w.h[:, :], dq_m.h[:, :], AF.Relu, bias=bm6, scale=1.0),
                reads=(dq_m.r, bm6r), writes=(dq_w.r,))
            b.op("gpsimd", lambda e: e.tensor_add(dq_r.h[:, :], dq_r.h[:, :], dq_w.h[:, :]),
                 reads=(dq_r.r, dq_w.r), writes=(dq_r.r,))
            b.op("vector", lambda e: e.scalar_tensor_tensor(
                dq_w.h[:, :], dq_m.h[:, :], 0.5, dq_r.h[:, :], ALU.mult, ALU.add),
                reads=(dq_m.r, dq_r.r), writes=(dq_w.r,))
            b.op("gpsimd", lambda e: e.tensor_scalar(
                dq_m.h[:, :], dq_s.h[:, :], -2.0, 1.0, ALU.mult, ALU.add),
                reads=(dq_s.r,), writes=(dq_m.r,))
            b.op("vector", lambda e: e.tensor_mul(dq_s.h[:, :], dq_w.h[:, :], dq_m.h[:, :]),
                 reads=(dq_w.r, dq_m.r), writes=(dq_s.r,))
            b.op("vector", lambda e, hh=hh: e.scalar_tensor_tensor(
                wd.h[:, hh * 2048:(hh + 1) * 2048], bsc.h[:, :], g128.h[:, 0:1],
                dq_s.h[:, :], ALU.mult, ALU.mult),
                reads=(bsc.r, g128.r, dq_s.r), writes=(wd.r,))
        for c in range(NC):
            ctx.transpose_tile(wd.h[:, c * 128:(c + 1) * 128], wd.r, wt, 0, c * 128,
                               pbank=2 + (c % 2))
        for th in range(2):
            pairs = [(wt.h[:, c * 128:(c + 1) * 128], ctx.xh[c].h[:, th * 512:(th + 1) * 512])
                     for c in range(NC)]
            pairs.append((bt_t.h[:, oc * 128:(oc + 1) * 128],
                          tr_f.h[:, th * 512:(th + 1) * 512]))

            def emit_mm(e, pairs=pairs, th=th):
                inst = None
                n = len(pairs)
                for i, (lh, rh) in enumerate(pairs):
                    inst = e.matmul(ctx.pb[th].h[:, :], lh, rh, start=(i == 0), stop=(i == n - 1))
                return inst
            b.op("tensor", emit_mm,
                 reads=tuple([wt.r, bt_t.r, tr_f.r] + [ctx.xh[c].r for c in range(NC)]),
                 writes=(ctx.pb[th].r,))
        evac_y_chunk(ctx, y_sb, oc, y_d, first=(oc == 0), last=(oc == NC - 1))



def oct_terms():
    TRIPLES = [(1, 2, 3), (1, 4, 5), (1, 7, 6), (2, 4, 6), (2, 5, 7), (3, 4, 7), (3, 6, 5)]
    S = np.zeros((8, 8, 8), np.float32)
    S[0, 0, 0] = 1.0
    for i in range(1, 8):
        S[0, i, i] = 1.0
        S[i, 0, i] = 1.0
        S[i, i, 0] = -1.0
    for a, bb, cc in TRIPLES:
        for x, y, z in ((a, bb, cc), (bb, cc, a), (cc, a, bb)):
            S[x, y, z] = 1.0
            S[y, x, z] = -1.0
    terms = []
    for k in range(8):
        tk = []
        for j in range(8):
            nz = np.nonzero(S[j, :, k])[0]
            tk.append((j, int(nz[0]), float(S[j, nz[0], k])))
        terms.append(tk)
    return terms


H8 = 1366
HIC = 11
D8 = 512


def hic_w(ic):
    return 128 if ic < 10 else 86


class TernaryScale:
    def __init__(self, nc, tag):
        self.s_col = sb(nc, f"ts_s_{tag}", [128, 1], F32)
        self.t_col = sb(nc, f"ts_t_{tag}", [128, 1], F32)
        self.nt_col = sb(nc, f"ts_nt_{tag}", [128, 1], F32)


def compute_ternary_scale(ctx, w_dram, n_rows, row_len, scale_obj, wload, acc, onef_col):
    b = ctx.b
    ntile = (n_rows + 127) // 128
    b.op("vector", lambda e: e.memset(acc.h[:, :], 0.0), reads=(), writes=(acc.r,))
    for it in range(ntile):
        p = min(128, n_rows - it * 128)
        b.dma("sync", lambda e, it=it, p=p: e.dma_start(
            out=wload.h[0:p, 0:row_len], in_=w_dram.a[it * 128:it * 128 + p, 0:row_len]),
            reads=(w_dram.r,), writes=(wload.r,))
        b.op("vector", lambda e, it=it, p=p: e.tensor_reduce(
            acc.h[0:p, it:it + 1], wload.h[0:p, 0:row_len], AX.X, ALU.add,
            apply_absolute_value=True),
            reads=(wload.r,), writes=(acc.r,))
    b.op("vector", lambda e: e.tensor_reduce(
        acc.h[:, ntile:ntile + 1], acc.h[:, 0:ntile], AX.X, ALU.add),
        reads=(acc.r,), writes=(acc.r,))
    b.op("tensor", lambda e: e.matmul(
        ctx.pb[3].h[0:1, 0:1], onef_col.h[:, 0:1], acc.h[:, ntile:ntile + 1],
        start=True, stop=True),
        reads=(acc.r, onef_col.r), writes=(ctx.pb[3].r,))
    inv_n = 1.0 / float(n_rows * row_len)
    b.op("vector", lambda e: e.tensor_single_scalar(
        ctx.row_c.h[0:1, 0:1], ctx.pb[3].h[0:1, 0:1], inv_n, ALU.mult),
        reads=(ctx.pb[3].r,), writes=(ctx.row_c.r,))
    b.op("tensor", lambda e: e.matmul(
        ctx.pb[3].h[:, 0:1], ctx.oner_f.h[0:1, :], ctx.row_c.h[0:1, 0:1],
        start=True, stop=True),
        reads=(ctx.row_c.r, ctx.oner_f.r), writes=(ctx.pb[3].r,))
    b.op("vector", lambda e: e.tensor_copy(scale_obj.s_col.h[:, :], ctx.pb[3].h[:, 0:1]),
         reads=(ctx.pb[3].r,), writes=(scale_obj.s_col.r,))
    b.op("vector", lambda e: e.tensor_single_scalar(
        scale_obj.t_col.h[:, :], scale_obj.s_col.h[:, :], 0.5, ALU.mult),
        reads=(scale_obj.s_col.r,), writes=(scale_obj.t_col.r,))
    b.op("vector", lambda e: e.tensor_single_scalar(
        scale_obj.nt_col.h[:, :], scale_obj.t_col.h[:, :], -1.0, ALU.mult),
        reads=(scale_obj.t_col.r,), writes=(scale_obj.nt_col.r,))


def quantize_tile(ctx, wload, p, L, sc, pos, neg, wq):
    b = ctx.b
    b.op("vector", lambda e: e.tensor_single_scalar(
        pos.h[0:p, 0:L], wload.h[0:p, 0:L], sc.t_col.h[0:p, 0:1], ALU.is_ge),
        reads=(wload.r, sc.t_col.r), writes=(pos.r,))
    b.op("gpsimd", lambda e: e.tensor_single_scalar(
        neg.h[0:p, 0:L], wload.h[0:p, 0:L], sc.nt_col.h[0:p, 0:1], ALU.is_le),
        reads=(wload.r, sc.nt_col.r), writes=(neg.r,))
    b.op("vector", lambda e: e.tensor_sub(pos.h[0:p, 0:L], pos.h[0:p, 0:L], neg.h[0:p, 0:L]),
         reads=(pos.r, neg.r), writes=(pos.r,))
    b.op("vector", lambda e: e.tensor_single_scalar(
        wq.h[0:p, 0:L], pos.h[0:p, 0:L], sc.s_col.h[0:p, 0:1], ALU.mult),
        reads=(pos.r, sc.s_col.r), writes=(wq.r,))


def emit_l0_ffn(ctx, io):
    nc, b = ctx.nc, ctx.b
    terms = oct_terms()

    wload = sb(nc, "g_wl", [128, 1408], F32)
    acc = sb(nc, "g_acc", [128, 96], F32)
    onef_col = sb(nc, "g_onef", [128, 1], F32)
    pos = sb(nc, "g_pos", [128, 1408], F32)
    neg = sb(nc, "g_neg", [128, 1408], F32)
    wq = sb(nc, "g_wq", [128, 1408], BF16)
    wt_g = [[sb(nc, f"g_wtg{j}_{s}", [128, 512], BF16) for s in range(2)] for j in range(8)]
    wt_u = [[sb(nc, f"g_wtu{j}_{s}", [128, 512], BF16) for s in range(2)] for j in range(8)]
    wt_d = [sb(nc, f"g_wtd{j}", [128, 1408], BF16) for j in range(8)]
    sg_sb = sb(nc, "g_sg", [128, 512], BF16)
    h_sb = sb(nc, "g_h", [128, 512], BF16)
    y_sb = sb(nc, "g_y", [128, 1024], F32)
    neg_sb = sb(nc, "g_negb", [128, 512], F32)

    sc_g = TernaryScale(nc, "g")
    sc_u = TernaryScale(nc, "u")
    sc_d = TernaryScale(nc, "d")

    w_gate, w_up, w_down, h_d, y_d = (io["w_gate"], io["w_up"], io["w_down"],
                                      io["h_d"], io["y_d"])

    b.op("vector", lambda e: e.memset(onef_col.h[:, :], 1.0), reads=(), writes=(onef_col.r,))

    compute_ternary_scale(ctx, w_gate, 8 * H8, D8, sc_g, wload, acc, onef_col)
    compute_ternary_scale(ctx, w_up, 8 * H8, D8, sc_u, wload, acc, onef_col)
    compute_ternary_scale(ctx, w_down, 8 * D8, H8, sc_d, wload, acc, onef_col)

    for ocp in range(HIC):
        ow = hic_w(ocp)
        for j in range(8):
            for mat, w_d, sc, wt in ((0, w_gate, sc_g, wt_g), (1, w_up, sc_u, wt_u)):
                r0 = j * H8 + ocp * 128
                b.dma("sync", lambda e, r0=r0, ow=ow, w_d=w_d: e.dma_start(
                    out=wload.h[0:ow, 0:D8], in_=w_d.a[r0:r0 + ow, 0:D8]),
                    reads=(w_d.r,), writes=(wload.r,))
                quantize_tile(ctx, wload, ow, D8, sc, pos, neg, wq)
                for ic in range(4):
                    ctx.transpose_tile(wq.h[0:ow, ic * 128:(ic + 1) * 128], wq.r,
                                       wt[j][0], 0, ic * 128, pbank=2 + (ic % 2),
                                       np_part=ow, n_cols=128)
                b.op("gpsimd", lambda e, j=j, wt=wt: e.tensor_single_scalar(
                    wt[j][1].h[:, :], wt[j][0].h[:, :], -1.0, ALU.mult),
                    reads=(wt[j][0].r,), writes=(wt[j][1].r,))
        for k in range(8):
            for th in range(2):
                for slot, wt in ((0, wt_g), (1, wt_u)):
                    pairs = []
                    rres = []
                    for (j, i_c, sgn) in terms[k]:
                        tile = wt[j][0 if sgn > 0 else 1]
                        rres.append(tile.r)
                        for ic in range(4):
                            pairs.append((tile.h[0:128, ic * 128:ic * 128 + ow],
                                          ctx.xh[4 * i_c + ic].h[:, th * 512:(th + 1) * 512]))
                            rres.append(ctx.xh[4 * i_c + ic].r)

                    def emit_mm(e, pairs=pairs, slot=slot, ow=ow):
                        inst = None
                        n = len(pairs)
                        for i, (lh, rh) in enumerate(pairs):
                            inst = e.matmul(ctx.pb[slot].h[0:ow, :], lh, rh,
                                            start=(i == 0), stop=(i == n - 1))
                        return inst
                    b.op("tensor", emit_mm, reads=tuple(rres), writes=(ctx.pb[slot].r,))
                b.op("scalar", lambda e, ow=ow: e.activation(
                    sg_sb.h[0:ow, :], ctx.pb[0].h[0:ow, :], AF.Silu),
                    reads=(ctx.pb[0].r,), writes=(sg_sb.r,))
                b.op("vector", lambda e, ow=ow: e.tensor_mul(
                    h_sb.h[0:ow, :], sg_sb.h[0:ow, :], ctx.pb[1].h[0:ow, :]),
                    reads=(sg_sb.r, ctx.pb[1].r), writes=(h_sb.r,))
                hr0 = k * H8 + ocp * 128
                b.dma("sync", lambda e, hr0=hr0, ow=ow, th=th: e.dma_start(
                    out=h_d.a[hr0:hr0 + ow, th * 512:(th + 1) * 512], in_=h_sb.h[0:ow, :]),
                    reads=(h_sb.r,), writes=(h_d.r,))

    wtd_d = io["wtd_d"]
    for j in range(8):
        for oc2 in range(4):
            r0 = j * D8 + oc2 * 128
            b.dma("sync", lambda e, r0=r0: e.dma_start(
                out=wload.h[0:128, 0:H8], in_=w_down.a[r0:r0 + 128, 0:H8]),
                reads=(w_down.r,), writes=(wload.r,))
            quantize_tile(ctx, wload, 128, H8, sc_d, pos, neg, wq)
            for ic in range(HIC):
                iw = hic_w(ic)
                ctx.transpose_tile(wq.h[0:128, ic * 128:ic * 128 + iw], wq.r,
                                   wt_d[j], 0, ic * 128, pbank=2 + (ic % 2),
                                   np_part=128, n_cols=iw)
            idx = j * 4 + oc2
            b.dma("sync", lambda e, idx=idx, j=j: e.dma_start(
                out=wtd_d.a[idx * 128:(idx + 1) * 128, :], in_=wt_d[j].h[:, :]),
                reads=(wt_d[j].r,), writes=(wtd_d.r,))

    donors = [(t, 2048) for t in ctx.xh]
    donors += [(t, 4096) for t in (ctx.xa[0], ctx.xa[1], ctx.ya[0], ctx.ya[1])]
    donors += [(pos, 5120), (neg, 5120)]
    slot_map = []
    for t, nbytes in donors:
        for k in range(nbytes // 1024):
            slot_map.append((t, k * 512))
    assert len(slot_map) >= 8 * HIC

    def h_slot(comp, ic):
        s = comp * HIC + ic
        t, off = slot_map[s]
        if t.h.dtype == BF16:
            return t, t.h[:, off:off + 512]
        return t, t.h[:, :].bitcast(BF16)[:, off:off + 512]

    for th in range(2):
        for j in range(8):
            for ic in range(HIC):
                iw = hic_w(ic)
                t, v = h_slot(j, ic)
                r0 = j * H8 + ic * 128
                b.dma("sync", lambda e, r0=r0, iw=iw, th=th, v=v: e.dma_start(
                    out=v[0:iw, :], in_=h_d.a[r0:r0 + iw, th * 512:(th + 1) * 512]),
                    reads=(h_d.r,), writes=(t.r,))
        for oc2 in range(4):
            for j in range(8):
                idx = j * 4 + oc2
                b.dma("sync", lambda e, j=j, idx=idx: e.dma_start(
                    out=wt_d[j].h[:, :], in_=wtd_d.a[idx * 128:(idx + 1) * 128, :]),
                    reads=(wtd_d.r,), writes=(wt_d[j].r,))
            for kp in range(8):
                gp = {1.0: [], -1.0: []}
                rres = []
                for (j, i_c, sgn) in terms[kp]:
                    rres.append(wt_d[j].r)
                    for ic in range(HIC):
                        iw = hic_w(ic)
                        t, v = h_slot(i_c, ic)
                        rres.append(t.r)
                        gp[sgn].append((wt_d[j].h[0:iw, ic * 128:(ic + 1) * 128],
                                        v[0:iw, :]))

                for slot, sgn in ((0, 1.0), (1, -1.0)):
                    pairs = gp[sgn]

                    def emit_mm(e, pairs=pairs, slot=slot):
                        inst = None
                        n = len(pairs)
                        for i, (lh, rh) in enumerate(pairs):
                            inst = e.matmul(ctx.pb[slot].h[:, :], lh, rh,
                                            start=(i == 0), stop=(i == n - 1))
                        return inst
                    b.op("tensor", emit_mm, reads=tuple(rres), writes=(ctx.pb[slot].r,))
                b.op("vector", lambda e: e.tensor_copy(neg_sb.h[:, :], ctx.pb[1].h[:, :]),
                     reads=(ctx.pb[1].r,), writes=(neg_sb.r,))
                b.op("vector", lambda e, th=th: e.tensor_sub(
                    y_sb.h[:, th * 512:(th + 1) * 512], ctx.pb[0].h[:, :], neg_sb.h[:, :]),
                    reads=(ctx.pb[0].r, neg_sb.r), writes=(y_sb.r,))
                first = (kp == 0 and oc2 == 0)
                last = (kp == 7 and oc2 == 3)
                c0 = th * 512
                cast, sq = ctx.sbf[0], ctx.sbf[1]
                b.op("vector", lambda e, c0=c0: e.tensor_copy(
                    cast.h[:, c0:c0 + 512], y_sb.h[:, c0:c0 + 512]),
                    reads=(y_sb.r,), writes=(cast.r,))
                b.op("scalar", lambda e, c0=c0: e.activation(
                    sq.h[:, c0:c0 + 512], y_sb.h[:, c0:c0 + 512], AF.Square),
                    reads=(y_sb.r,), writes=(sq.r,))
                ts, vs = st_sum_view(ctx, c0)
                tq, vq = st_sq_view(ctx, c0)
                b.op("tensor", lambda e, c0=c0, first=first, last=last, vs=vs: e.matmul(
                    vs, ctx.onec_bf.h[:, 0:1],
                    cast.h[:, c0:c0 + 512], start=first, stop=last),
                    reads=(cast.r, ctx.onec_bf.r), writes=(ts.r,))
                b.op("tensor", lambda e, c0=c0, first=first, last=last, vq=vq: e.matmul(
                    vq, ctx.onec_bf.h[:, 0:1],
                    sq.h[:, c0:c0 + 512], start=first, stop=last),
                    reads=(sq.r, ctx.onec_bf.r), writes=(tq.r,))
                yr0 = kp * 512 + oc2 * 128
                b.dma("sync", lambda e, yr0=yr0, th=th: e.dma_start(
                    out=y_d.a[yr0:yr0 + 128, th * 512:(th + 1) * 512],
                    in_=y_sb.h[:, th * 512:(th + 1) * 512]),
                    reads=(y_sb.r,), writes=(y_d.r,))



def emit_l0_attn(ctx, io):
    nc, b = ctx.nc, ctx.b

    identf = sb(nc, "a_idf", [128, 128], F32)
    wnat = [sb(nc, f"a_wn{i}", [128, 4096], BF16) for i in range(2)]
    wvt = sb(nc, "a_wvt", [128, 16384], BF16)
    wqe = sb(nc, "a_wqe", [128, 8192], BF16)
    ptc = [sb(nc, f"a_pt{i}", [128, 2 * 256], BF16) for i in range(4)]
    pth = sb(nc, "a_pth", [128, 2 * 256], BF16)
    bw_cols = sb(nc, "a_bw", [128, 48], F32)
    bqc_bf = sb(nc, "a_bqc", [128, 32], BF16)
    beff_cols = sb(nc, "a_beff", [128, 32], F32)
    bk_cols = sb(nc, "a_bk", [128, 32], F32)
    bv_row = sb(nc, "a_bv", [1, 4096], BF16)
    eva = sb(nc, "a_eva", [128, 1024], BF16)
    maxc = sb(nc, "a_maxc", [128, 8], F32)
    negm = sb(nc, "a_negm", [128, 1], F32)
    sumc = sb(nc, "a_sumc", [128, 8], F32)
    sums8 = sb(nc, "a_sums", [128, 8], F32)
    rec8 = sb(nc, "a_rec", [128, 8], F32)
    rrows = sb(nc, "a_rr", [8, 128], F32)
    rrow = sb(nc, "a_row", [1, 1024], F32)
    colsc = sb(nc, "a_colsc", [128, 1], F32)

    k_h = [T(None, "") for _ in range(2)]
    k_h[0].h, k_h[0].r = wvt.h, wvt.r
    k_h[1].h, k_h[1].r = wvt.h, wvt.r
    kh_ap = [wvt.h[:, 0:4096], wvt.h[:, 4096:8192]]
    vh_ap = wvt.h[:, 8192:16384]
    qh_ap = [wqe.h[:, 0:1024], wqe.h[:, 1024:2048]]
    psb_ap = wqe.h[:, 2048:6144]
    y_sb_t = wnat[0]
    y_sb = wnat[0].h[:, :].bitcast(F32)[:, 0:1024]

    (wq_d, wk_d, wv_d) = (io["wq"], io["wk"], io["wv"])
    (bq_d, bk_d, bv_d) = (io["bq"], io["bk"], io["bv"])
    bw_d = io["bw"]
    pt_d = [io["ptd"], io["ptc0"], io["ptc1"], io["ptc2"]]
    q_d, k_own, v_own = io["q_d"], io["k_own"], io["v_own"]
    kg_d, vg_d, y_d, fl8_d = io["kg"], io["vg"], io["y_d"], io["fl8"]
    identf_d = io["identf"]

    b.dma("sync", lambda e: e.dma_start(out=identf.h[:, :], in_=identf_d.a[:, :]),
          reads=(identf_d.r,), writes=(identf.r,))
    for i in range(4):
        b.dma("gpsimd", lambda e, i=i: e.dma_start(
            out=ptc[i].h[:, 0:256], in_=pt_d[i].a[0:128, :]),
            reads=(pt_d[i].r,), writes=(ptc[i].r,))
        b.dma("gpsimd", lambda e, i=i: e.dma_start(
            out=ptc[i].h[:, 256:512], in_=pt_d[i].a[128:256, :]),
            reads=(pt_d[i].r,), writes=(ptc[i].r,))
    b.dma("sync", lambda e: e.dma_start(out=rrow.h[0:1, 0:48], in_=bw_d.a[:, :]),
          reads=(bw_d.r,), writes=(rrow.r,))
    b.op("tensor", lambda e: e.matmul(ctx.pb[2].h[:, 0:48], ctx.oner_f.h[0:1, :],
                                      rrow.h[0:1, 0:48], start=True, stop=True),
         reads=(rrow.r, ctx.oner_f.r), writes=(ctx.pb[2].r,))
    b.op("vector", lambda e: e.tensor_copy(bw_cols.h[:, :], ctx.pb[2].h[:, 0:48]),
         reads=(ctx.pb[2].r,), writes=(bw_cols.r,))
    for cc in range(NC):
        b.dma("sync", lambda e, cc=cc: e.dma_start(
            out=bk_cols.h[:, cc:cc + 1], in_=bq_d.a[cc * 128:(cc + 1) * 128].unsqueeze(1)),
            reads=(bq_d.r,), writes=(bk_cols.r,))
    b.op("vector", lambda e: e.tensor_copy(bqc_bf.h[:, :], bk_cols.h[:, :]),
         reads=(bk_cols.r,), writes=(bqc_bf.r,))
    for cc in range(NC):
        b.dma("sync", lambda e, cc=cc: e.dma_start(
            out=bk_cols.h[:, cc:cc + 1], in_=bk_d.a[cc * 128:(cc + 1) * 128].unsqueeze(1)),
            reads=(bk_d.r,), writes=(bk_cols.r,))
    b.dma("gpsimd", lambda e: e.dma_start(out=bv_row.h[:, :], in_=bv_d.a[:].unsqueeze(0)),
          reads=(bv_d.r,), writes=(bv_row.r,))

    for h in range(16):
        for fc in range(2):
            sl = slice(fc * 256, fc * 256 + 256)
            b.op("vector", lambda e, h=h, sl=sl: e.scalar_tensor_tensor(
                pth.h[:, sl], ptc[1].h[:, sl], bw_cols.h[:, 3 * h:3 * h + 1],
                ptc[0].h[:, sl], ALU.mult, ALU.add),
                reads=(ptc[0].r, ptc[1].r, bw_cols.r), writes=(pth.r,))
            b.op("vector", lambda e, h=h, sl=sl: e.scalar_tensor_tensor(
                pth.h[:, sl], ptc[2].h[:, sl], bw_cols.h[:, 3 * h + 1:3 * h + 2],
                pth.h[:, sl], ALU.mult, ALU.add),
                reads=(ptc[2].r, bw_cols.r, pth.r), writes=(pth.r,))
            b.op("vector", lambda e, h=h, sl=sl: e.scalar_tensor_tensor(
                pth.h[:, sl], ptc[3].h[:, sl], bw_cols.h[:, 3 * h + 2:3 * h + 3],
                pth.h[:, sl], ALU.mult, ALU.add),
                reads=(ptc[3].r, bw_cols.r, pth.r), writes=(pth.r,))
        for u in range(2):
            b.op("tensor", lambda e, h=h, u=u: (
                e.matmul(ctx.pb[3].h[:, 0:1], pth.h[:, u * 128:(u + 1) * 128],
                         bqc_bf.h[:, 2 * h:2 * h + 1], start=True, stop=False),
                e.matmul(ctx.pb[3].h[:, 0:1], pth.h[:, 256 + u * 128:256 + (u + 1) * 128],
                         bqc_bf.h[:, 2 * h + 1:2 * h + 2], start=False, stop=True))[1],
                reads=(pth.r, bqc_bf.r), writes=(ctx.pb[3].r,))
            b.op("vector", lambda e, h=h, u=u: e.tensor_copy(
                beff_cols.h[:, 2 * h + u:2 * h + u + 1], ctx.pb[3].h[:, 0:1]),
                reads=(ctx.pb[3].r,), writes=(beff_cols.r,))
        for fc in range(2):
            r0 = h * 256 + fc * 128
            b.dma("gpsimd", lambda e, r0=r0, fc=fc: e.dma_start(
                out=wnat[fc].h[:, :], in_=wq_d.a[r0:r0 + 128, :]),
                reads=(wq_d.r,), writes=(wnat[fc].r,))
        for ic in range(NC):
            pbk = ctx.pb[2 + (ic % 2)]
            b.op("tensor", lambda e, ic=ic, pbk=pbk: (
                e.matmul(pbk.h[:, 0:256], wnat[0].h[:, ic * 128:(ic + 1) * 128],
                         pth.h[:, 0:256], start=True, stop=False),
                e.matmul(pbk.h[:, 0:256], wnat[1].h[:, ic * 128:(ic + 1) * 128],
                         pth.h[:, 256:512], start=False, stop=True))[1],
                reads=(wnat[0].r, wnat[1].r, pth.r), writes=(pbk.r,))
            b.op("vector", lambda e, ic=ic, pbk=pbk: e.tensor_copy(
                wqe.h[:, ic * 256:(ic + 1) * 256], pbk.h[:, 0:256]),
                reads=(pbk.r,), writes=(wqe.r,))
        for u in range(2):
            oc = 2 * h + u
            for th in range(2):
                pbk = ctx.pb[th]

                def emit_mm(e, u=u, th=th, pbk=pbk):
                    inst = None
                    for ic in range(NC):
                        inst = e.matmul(pbk.h[:, :],
                                        wqe.h[:, ic * 256 + u * 128:ic * 256 + (u + 1) * 128],
                                        ctx.xh[ic].h[:, th * 512:(th + 1) * 512],
                                        start=(ic == 0), stop=(ic == NC - 1))
                    return inst
                b.op("tensor", emit_mm,
                     reads=tuple([wqe.r] + [t.r for t in ctx.xh]), writes=(pbk.r,))
                b.op("vector", lambda e, oc=oc, th=th, pbk=pbk: e.tensor_single_scalar(
                    eva.h[:, th * 512:(th + 1) * 512], pbk.h[:, :],
                    beff_cols.h[:, oc:oc + 1], ALU.add),
                    reads=(pbk.r, beff_cols.r), writes=(eva.r,))
            b.dma("sync", lambda e, oc=oc: e.dma_start(
                out=q_d.a[oc * 128:(oc + 1) * 128, :], in_=eva.h[:, :]),
                reads=(eva.r,), writes=(q_d.r,))

    wkt_ap = wvt.h[:, 0:4096]
    for oc in range(NC):
        b.dma("gpsimd", lambda e, oc=oc: e.dma_start(
            out=wnat[0].h[:, :], in_=wk_d.a[oc * 128:(oc + 1) * 128, :]),
            reads=(wk_d.r,), writes=(wnat[0].r,))
        for ic in range(NC):
            ctx.transpose_tile(wnat[0].h[:, ic * 128:(ic + 1) * 128], wnat[0].r,
                               wvt, 0, ic * 128, pbank=2 + (ic % 2))
        for th in range(2):
            pbk = ctx.pb[th]

            def emit_mm(e, th=th, pbk=pbk):
                inst = None
                for ic in range(NC):
                    inst = e.matmul(pbk.h[:, :], wkt_ap[:, ic * 128:(ic + 1) * 128],
                                    ctx.xh[ic].h[:, th * 512:(th + 1) * 512],
                                    start=(ic == 0), stop=(ic == NC - 1))
                return inst
            b.op("tensor", emit_mm,
                 reads=tuple([wvt.r] + [t.r for t in ctx.xh]), writes=(pbk.r,))
            b.op("vector", lambda e, oc=oc, th=th, pbk=pbk: e.tensor_single_scalar(
                eva.h[:, th * 512:(th + 1) * 512], pbk.h[:, :],
                bk_cols.h[:, oc:oc + 1], ALU.add),
                reads=(pbk.r, bk_cols.r), writes=(eva.r,))
        b.dma("sync", lambda e, oc=oc: e.dma_start(
            out=k_own.a[oc * 128:(oc + 1) * 128, :], in_=eva.h[:, :]),
            reads=(eva.r,), writes=(k_own.r,))

    for ocq in range(8):
        for q4 in range(4):
            oc = 4 * ocq + q4
            b.dma("gpsimd", lambda e, oc=oc: e.dma_start(
                out=wnat[0].h[:, :], in_=wv_d.a[oc * 128:(oc + 1) * 128, :]),
                reads=(wv_d.r,), writes=(wnat[0].r,))
            for ic in range(NC):
                ctx.transpose_tile(wnat[0].h[:, ic * 128:(ic + 1) * 128], wnat[0].r,
                                   wvt, 0, ic * 512 + q4 * 128, pbank=2 + (ic % 2))
        for tt in range(NT):
            pbk = ctx.pb[tt % 2]

            def emit_mm(e, tt=tt, ocq=ocq, pbk=pbk):
                inst = None
                for ic in range(NC):
                    inst = e.matmul(pbk.h[:, :], ctx.xh[ic].h[:, tt * 128:(tt + 1) * 128],
                                    wvt.h[:, ic * 512:(ic + 1) * 512],
                                    start=(ic == 0), stop=False)
                inst = e.matmul(pbk.h[:, :], ctx.oner_bf.h[0:1, 0:128],
                                bv_row.h[0:1, ocq * 512:(ocq + 1) * 512],
                                start=False, stop=True)
                return inst
            b.op("tensor", emit_mm,
                 reads=tuple([wvt.r, bv_row.r, ctx.oner_bf.r] + [t.r for t in ctx.xh]),
                 writes=(pbk.r,))
            b.op("vector", lambda e, tt=tt, pbk=pbk: e.tensor_copy(
                eva.h[:, 0:512], pbk.h[:, :]),
                reads=(pbk.r,), writes=(eva.r,))
            b.dma("sync", lambda e, tt=tt, ocq=ocq: e.dma_start(
                out=v_own.a[tt * 128:(tt + 1) * 128, ocq * 512:(ocq + 1) * 512],
                in_=eva.h[:, 0:512]),
                reads=(eva.r,), writes=(v_own.r,))

    groups = [[0, 1, 2, 3], [4, 5, 6, 7]]
    b.collective(lambda e: e.collective_compute(
        "AllGather", ALU.bypass, replica_groups=groups,
        ins=[k_own.a[:, :].opt()], outs=[kg_d.a[:, :].opt()]),
        reads=(k_own.r,), writes=(kg_d.r,))
    b.collective(lambda e: e.collective_compute(
        "AllGather", ALU.bypass, replica_groups=groups,
        ins=[v_own.a[:, :].opt()], outs=[vg_d.a[:, :].opt()]),
        reads=(v_own.r,), writes=(vg_d.r,))

    for h in range(16):
        for dc in range(2):
            src = kg_d.a.rearrange("(r f) t -> f r t", r=4)[h * 256 + dc * 128:
                                                            h * 256 + (dc + 1) * 128, :, :]
            b.dma("sync", lambda e, dc=dc, src=src: e.dma_start(out=kh_ap[dc].rearrange(
                "p (r t) -> p r t", r=4), in_=src),
                reads=(kg_d.r,), writes=(wvt.r,))
        vsrc = vg_d.a.rearrange("(g p) v -> p g v", p=128)[:, :, h * 256:(h + 1) * 256]
        b.dma("sync", lambda e, vsrc=vsrc: e.dma_start(
            out=vh_ap.rearrange("p (g v) -> p g v", g=32), in_=vsrc),
            reads=(vg_d.r,), writes=(wvt.r,))
        for dc in range(2):
            r0 = h * 256 + dc * 128
            b.dma("sync", lambda e, r0=r0, dc=dc: e.dma_start(
                out=qh_ap[dc], in_=q_d.a[r0:r0 + 128, :]),
                reads=(q_d.r,), writes=(wqe.r,))
        for qt in range(NT):
            def emit_sc(e, qt=qt):
                inst = None
                for dc in range(2):
                    for kc in range(8):
                        inst = e.matmul(ctx.pb[kc].h[:, :],
                                        qh_ap[dc][:, qt * 128:(qt + 1) * 128],
                                        kh_ap[dc][:, kc * 512:(kc + 1) * 512],
                                        start=(dc == 0), stop=(dc == 1))
                return inst
            b.op("tensor", emit_sc, reads=(wqe.r, wvt.r),
                 writes=tuple(ctx.pb[i].r for i in range(8)))
            for kc in range(8):
                b.op("vector", lambda e, kc=kc: e.reduce_max(
                    maxc.h[:, kc:kc + 1], ctx.pb[kc].h[:, :], AX.X),
                    reads=(ctx.pb[kc].r,), writes=(maxc.r,))
            b.op("vector", lambda e: e.tensor_reduce(
                negm.h[:, :], maxc.h[:, :], AX.X, ALU.max, negate=True),
                reads=(maxc.r,), writes=(negm.r,))
            for kc in range(8):
                b.op("scalar", lambda e, kc=kc: e.activation(
                    psb_ap[:, kc * 512:(kc + 1) * 512], ctx.pb[kc].h[:, :], AF.Exp,
                    bias=negm.h[:, 0:1], accum_out=sumc.h[:, kc:kc + 1]),
                    reads=(ctx.pb[kc].r, negm.r), writes=(wqe.r, sumc.r))
            b.op("vector", lambda e, qt=qt: e.tensor_reduce(
                sums8.h[:, qt:qt + 1], sumc.h[:, :], AX.X, ALU.add),
                reads=(sumc.r,), writes=(sums8.r,))
            for g in range(NC):
                ctx.transpose_tile(psb_ap[:, g * 128:(g + 1) * 128], wqe.r,
                                   ctx.xh[g], 0, qt * 128, pbank=2 + (g % 2))
        def emit_av(e):
            inst = None
            for g in range(NC):
                for vc in range(2):
                    for th in range(2):
                        inst = e.matmul(
                            ctx.pb[2 * vc + th].h[:, :],
                            vh_ap[:, g * 256 + vc * 128:g * 256 + (vc + 1) * 128],
                            ctx.xh[g].h[:, th * 512:(th + 1) * 512],
                            start=(g == 0), stop=(g == NC - 1))
            return inst
        b.op("tensor", emit_av,
             reads=tuple([wvt.r] + [t.r for t in ctx.xh]),
             writes=tuple(ctx.pb[i].r for i in range(4)))
        b.op("vector", lambda e: e.reciprocal(rec8.h[:, :], sums8.h[:, :]),
             reads=(sums8.r,), writes=(rec8.r,))
        b.op("tensor", lambda e: e.transpose(ctx.pb[4].h[0:8, 0:128], rec8.h[:, :],
                                             identf.h[:, :]),
             reads=(rec8.r, identf.r), writes=(ctx.pb[4].r,))
        b.op("vector", lambda e: e.tensor_copy(rrows.h[:, :], ctx.pb[4].h[0:8, 0:128]),
             reads=(ctx.pb[4].r,), writes=(rrows.r,))
        b.dma("sync", lambda e: e.dma_start(out=fl8_d.a[:, :], in_=rrows.h[:, :]),
              reads=(rrows.r,), writes=(fl8_d.r,))
        b.dma("sync", lambda e: e.dma_start(
            out=rrow.h[0:1, 0:1024], in_=fl8_d.a[:, :].rearrange("p c -> (p c)").unsqueeze(0)),
            reads=(fl8_d.r,), writes=(rrow.r,))
        ctx.bcast_row(rrow, ctx.r_b, pbank=5)
        for vc in range(2):
            for th in range(2):
                b.op("vector", lambda e, vc=vc, th=th: e.tensor_mul(
                    y_sb[:, th * 512:(th + 1) * 512], ctx.pb[2 * vc + th].h[:, :],
                    ctx.r_b.h[:, th * 512:(th + 1) * 512]),
                    reads=(ctx.pb[2 * vc + th].r, ctx.r_b.r), writes=(y_sb_t.r,))
            yr0 = h * 256 + vc * 128
            b.dma("sync", lambda e, yr0=yr0: e.dma_start(
                out=y_d.a[yr0:yr0 + 128, :], in_=y_sb),
                reads=(y_sb_t.r,), writes=(y_d.r,))


def ga_pt_consts():
    dim, n = 3, 8
    masks = sorted(range(n), key=lambda m: (bin(m).count("1"), m))
    inv = {m: i for i, m in enumerate(masks)}
    C = np.zeros((n, n, n), dtype=np.float32)
    for i, a in enumerate(masks):
        for j, bb in enumerate(masks):
            sw, at = 0, a >> 1
            while at:
                sw += bin(at & bb).count("1")
                at >>= 1
            s = -1.0 if sw & 1 else 1.0
            C[i, j, inv[a ^ bb]] = s
    grades = np.array([bin(m).count("1") for m in masks])
    rev = np.where(grades * (grades - 1) // 2 % 2 == 1, -1.0, 1.0).astype(np.float32)
    BIV = [4, 5, 6]
    scale = 1.0 / 16.0
    def blockdiag(B8):
        M = np.zeros((256, 256), np.float32)
        for d in range(32):
            M[d * 8:(d + 1) * 8, d * 8:(d + 1) * 8] = B8
        return M
    ptd = blockdiag(np.diag(rev) * scale)
    ptcs = [blockdiag(rev[:, None] * C[:, :, bi]) for bi in BIV]
    return ptd, ptcs[0], ptcs[1], ptcs[2]



L1H = 32
L1_SCALE = (128.0 ** -0.5) * (0.1 * math.log(128.0) + 1.0) ** 2


def yarn_cos_sin(seq_len=SEQ, dim=64, base=10000.0, factor=128.0, beta_fast=32.0,
                 beta_slow=1.0, orig_max=8192, mscale=1.0, mscale_all_dim=1.0):
    def corr_dim(r):
        return dim * math.log(orig_max / (r * 2 * math.pi)) / (2 * math.log(base))
    low = max(math.floor(corr_dim(beta_fast)), 0)
    high = min(math.ceil(corr_dim(beta_slow)), dim - 1)
    hi = high + 0.001 if low == high else float(high)
    ramp = np.clip((np.arange(dim // 2, dtype=np.float32) - low) / (hi - low), 0.0, 1.0)
    inv_freq_mask = 1.0 - ramp
    freq_extra = 1.0 / base ** (np.arange(0, dim, 2, dtype=np.float32) / dim)
    freq_inter = freq_extra / factor
    inv_freq = freq_inter * (1.0 - inv_freq_mask) + freq_extra * inv_freq_mask
    t = np.arange(seq_len, dtype=np.float32)
    freqs = np.outer(t, inv_freq)

    def gm(s, m):
        return 1.0 if s <= 1 else 0.1 * m * math.log(s) + 1.0
    ms = gm(factor, mscale) / gm(factor, mscale_all_dim)
    return np.cos(freqs) * ms, np.sin(freqs) * ms


def emit_rope_tm(ctx, src_ap, src_res, dst_tile, dst_col0, cst, cst_res, nope_scale,
                 n_heads, head_stride):
    b = ctx.b
    tmp1 = ctx.rope_t1
    tmp2 = ctx.rope_t2
    for hh in range(n_heads):
        c0 = hh * head_stride
        d0 = dst_col0 + c0
        if nope_scale is None:
            b.op("vector", lambda e, c0=c0, d0=d0: e.tensor_copy(
                dst_tile.h[:, d0:d0 + 64], src_ap[:, c0:c0 + 64]),
                reads=(src_res,), writes=(dst_tile.r,))
        else:
            b.op("vector", lambda e, c0=c0, d0=d0: e.tensor_single_scalar(
                dst_tile.h[:, d0:d0 + 64], src_ap[:, c0:c0 + 64], nope_scale, ALU.mult),
                reads=(src_res,), writes=(dst_tile.r,))
        e_ap = src_ap[:, c0 + 64:c0 + 128:2]
        o_ap = src_ap[:, c0 + 65:c0 + 128:2]
        cos_ap = cst.h[:, 0:32]
        sin_ap = cst.h[:, 32:64]
        b.op("vector", lambda e, e_ap=e_ap: e.tensor_mul(tmp1.h[:, 0:32], e_ap, cos_ap),
             reads=(src_res, cst_res), writes=(tmp1.r,))
        b.op("vector", lambda e, o_ap=o_ap: e.tensor_mul(tmp2.h[:, 0:32], o_ap, sin_ap),
             reads=(src_res, cst_res), writes=(tmp2.r,))
        b.op("vector", lambda e, d0=d0: e.tensor_sub(
            dst_tile.h[:, d0 + 64:d0 + 96], tmp1.h[:, 0:32], tmp2.h[:, 0:32]),
            reads=(tmp1.r, tmp2.r), writes=(dst_tile.r,))
        b.op("vector", lambda e, o_ap=o_ap: e.tensor_mul(tmp1.h[:, 0:32], o_ap, cos_ap),
             reads=(src_res, cst_res), writes=(tmp1.r,))
        b.op("vector", lambda e, e_ap=e_ap: e.tensor_mul(tmp2.h[:, 0:32], e_ap, sin_ap),
             reads=(src_res, cst_res), writes=(tmp2.r,))
        b.op("vector", lambda e, d0=d0: e.tensor_add(
            dst_tile.h[:, d0 + 96:d0 + 128], tmp1.h[:, 0:32], tmp2.h[:, 0:32]),
            reads=(tmp1.r, tmp2.r), writes=(dst_tile.r,))


def emit_l1_attn(ctx, io, core_j):
    nc, b = ctx.nc, ctx.b

    wvt = sb(nc, "m_wvt", [128, 16384], BF16)
    wqe = sb(nc, "m_wqe", [128, 8192], BF16)
    wnat = sb(nc, "m_wn", [128, 4096], BF16)
    qatm = [sb(nc, f"m_qa{i}", [128, 1024], BF16) for i in range(NT)]
    qtm = sb(nc, "m_qtm", [128, 512], F32)
    ckvt = sb(nc, "m_ckv", [128, 320], F32)
    ckvnf = sb(nc, "m_ckf", [128, 2 * 1024], BF16)
    kpef = sb(nc, "m_kpe", [64, 1024], BF16)
    wknat = sb(nc, "m_wkn", [128, 256], BF16)
    wkbt = sb(nc, "m_wkb", [128, 256], BF16)
    eva = sb(nc, "m_eva", [128, 1024], BF16)
    lnw_b = sb(nc, "m_lnw", [128, 1024], F32)
    kvlnw_b = sb(nc, "m_kvw", [128, 256], F32)
    cst_q = sb(nc, "m_csq", [128, 64], F32)
    cst_k = sb(nc, "m_csk", [128, 64], F32)
    col_a = sb(nc, "m_ca", [128, 1], F32)
    col_b = sb(nc, "m_cb", [128, 1], F32)
    tri = sb(nc, "m_tri", [128, 128], F32)
    cvrow = sb(nc, "m_cv", [1, 4096], BF16)
    kdiag, vdiag, pdiag = qatm[0], qatm[1], qatm[2]
    sc_diag = sb(nc, "m_scd", [128, 128], F32)
    maxc = sb(nc, "m_mx", [128, 9], F32)
    negm = sb(nc, "m_nm", [128, 1], F32)
    sumc = sb(nc, "m_sc", [128, 9], F32)
    sums8 = sb(nc, "m_s8", [128, 8], F32)
    rec8 = sb(nc, "m_r8", [128, 8], F32)
    rrows = sb(nc, "m_rr", [8, 128], F32)
    rrow = T(None, "")
    rrow.h, rrow.r = ckvnf.h, ckvnf.r
    rrow_ap = ckvnf.h[:, :].bitcast(F32)[0:1, 0:1024]
    identf = sb(nc, "m_idf", [128, 128], F32)
    evaf = sb(nc, "m_evf", [128, 128], F32)
    ctx.rope_t1 = sb(nc, "m_rt1", [128, 32], F32)
    ctx.rope_t2 = sb(nc, "m_rt2", [128, 32], F32)
    qanf = []
    for t in (ctx.xa[0], ctx.xa[1], ctx.ya[0], ctx.ya[1]):
        v = t.h[:, :].bitcast(BF16)
        qanf.append((t, v[:, 0:1024]))
        qanf.append((t, v[:, 1024:2048]))

    kh_ap = wvt.h[:, 0:4096]
    vh_ap = wvt.h[:, 4096:8192]
    qh_ap = wqe.h[:, 0:1024]
    psb_ap = wqe.h[:, 2048:6272]
    aot = T(None, "")
    aot.h, aot.r = wqe.h, wqe.r
    aot_ap = wqe.h[:, 0:4096]
    ytile = T(None, "")
    ytile.h, ytile.r = wvt.h, wvt.r
    ytile_ap = wvt.h[:, :].bitcast(F32)[:, 0:4096]

    wq_a, wq_b, wkv_a, wkv_b, wo_d = (io["wq_a"], io["wq_b"], io["wkv_a"],
                                      io["wkv_b"], io["wo"])
    qlnw_d, kvlnw_d = io["q_a_ln_w"], io["kv_a_ln_w"]
    cs_q_d, cs_k_d, tri_d, cvals_d = io["cs_q"], io["cs_k"], io["tri"], io["cvals"]
    q_d, kfm_d, v_own = io["q_d"], io["kfm"], io["v_own"]
    kg_d, vg_d = io["kg"], io["vg"]
    ao_d, y_d, lny_d, fl8_d = io["ao_d"], io["y_d"], io["lny_d"], io["fl8"]
    identf_d = io["identf"]

    b.dma("sync", lambda e: e.dma_start(out=identf.h[:, :], in_=identf_d.a[:, :]),
          reads=(identf_d.r,), writes=(identf.r,))
    b.dma("sync", lambda e: e.dma_start(out=tri.h[:, :], in_=tri_d.a[:, :]),
          reads=(tri_d.r,), writes=(tri.r,))
    b.dma("sync", lambda e: e.dma_start(out=rrow_ap, in_=qlnw_d.a[:].unsqueeze(0)),
          reads=(qlnw_d.r,), writes=(rrow.r,))
    ctx.bcast_row(rrow, lnw_b, pbank=0, row_ap=rrow_ap)
    b.dma("sync", lambda e: e.dma_start(out=rrow_ap[0:1, 0:256], in_=kvlnw_d.a[:].unsqueeze(0)),
          reads=(kvlnw_d.r,), writes=(rrow.r,))
    for j in range(0, 256, 512):
        pass
    b.op("tensor", lambda e: e.matmul(ctx.pb[0].h[:, 0:256], ctx.oner_f.h[0:1, :],
                                      rrow_ap[0:1, 0:256], start=True, stop=True),
         reads=(rrow.r, ctx.oner_f.r), writes=(ctx.pb[0].r,))
    b.op("vector", lambda e: e.tensor_copy(kvlnw_b.h[:, :], ctx.pb[0].h[:, 0:256]),
         reads=(ctx.pb[0].r,), writes=(kvlnw_b.r,))

    eps_rms, eps_rms_r = ctx.cst(EPS_RMS)

    for oq in range(2):
        for q4 in range(4):
            oc = 4 * oq + q4
            b.dma("gpsimd", lambda e, oc=oc: e.dma_start(
                out=wnat.h[:, :], in_=wq_a.a[oc * 128:(oc + 1) * 128, :]),
                reads=(wq_a.r,), writes=(wnat.r,))
            for ic in range(NC):
                ctx.transpose_tile(wnat.h[:, ic * 128:(ic + 1) * 128], wnat.r,
                                   wvt, 0, ic * 512 + q4 * 128, pbank=2 + (ic % 2))
        for tt in range(NT):
            pbk = ctx.pb[tt % 2]

            def emit_mm(e, tt=tt, pbk=pbk):
                inst = None
                for ic in range(NC):
                    inst = e.matmul(pbk.h[:, :], ctx.xh[ic].h[:, tt * 128:(tt + 1) * 128],
                                    wvt.h[:, ic * 512:(ic + 1) * 512],
                                    start=(ic == 0), stop=(ic == NC - 1))
                return inst
            b.op("tensor", emit_mm,
                 reads=tuple([wvt.r] + [t.r for t in ctx.xh]), writes=(pbk.r,))
            b.op("vector", lambda e, tt=tt, oq=oq, pbk=pbk: e.tensor_copy(
                qatm[tt].h[:, oq * 512:(oq + 1) * 512], pbk.h[:, :]),
                reads=(pbk.r,), writes=(qatm[tt].r,))
    for tt in range(NT):
        b.op("scalar", lambda e, tt=tt: e.activation(
            ctx.sbf[0].h[:, 0:1024], qatm[tt].h[:, :], AF.Square,
            accum_out=col_a.h[:, 0:1]),
            reads=(qatm[tt].r,), writes=(ctx.sbf[0].r, col_a.r))
        b.op("vector", lambda e: e.tensor_single_scalar(
            col_a.h[:, :], col_a.h[:, :], 1.0 / 1024.0, ALU.mult),
            reads=(col_a.r,), writes=(col_a.r,))
        b.op("scalar", lambda e: e.activation(col_a.h[:, :], col_a.h[:, :], AF.Sqrt,
                                              bias=eps_rms),
             reads=(col_a.r, eps_rms_r), writes=(col_a.r,))
        b.op("vector", lambda e: e.reciprocal(col_a.h[:, :], col_a.h[:, :]),
             reads=(col_a.r,), writes=(col_a.r,))
        b.op("vector", lambda e, tt=tt: e.tensor_single_scalar(
            ctx.sbf[0].h[:, 0:1024], qatm[tt].h[:, :], col_a.h[:, 0:1], ALU.mult),
            reads=(qatm[tt].r, col_a.r), writes=(ctx.sbf[0].r,))
        b.op("vector", lambda e, tt=tt: e.tensor_mul(
            ctx.sbf[1].h[:, 0:1024], ctx.sbf[0].h[:, 0:1024], lnw_b.h[:, :]),
            reads=(ctx.sbf[0].r, lnw_b.r), writes=(ctx.sbf[1].r,))
        for ic2 in range(8):
            tgt, tv = qanf[ic2]
            pview = ctx.pb[2 + (ic2 % 2)].h[:, :].bitcast(BF16)
            b.op("tensor", lambda e, ic2=ic2, pview=pview: e.transpose(
                pview[0:128, 0:128], ctx.sbf[1].h[:, ic2 * 128:(ic2 + 1) * 128],
                ctx.ident.h[:, :]),
                reads=(ctx.sbf[1].r, ctx.ident.r), writes=(ctx.pb[2 + (ic2 % 2)].r,))
            b.op("vector", lambda e, ic2=ic2, tt=tt, tgt=tgt, tv=tv, pview=pview:
                 e.tensor_copy(tv[:, tt * 128:(tt + 1) * 128], pview[0:128, 0:128]),
                 reads=(ctx.pb[2 + (ic2 % 2)].r,), writes=(tgt.r,))

    for oq in range(8):
        for q4 in range(4):
            oc = 4 * oq + q4
            b.dma("gpsimd", lambda e, oc=oc: e.dma_start(
                out=wnat.h[:, 0:1024], in_=wq_b.a[oc * 128:(oc + 1) * 128, :]),
                reads=(wq_b.r,), writes=(wnat.r,))
            for ic2 in range(8):
                ctx.transpose_tile(wnat.h[:, ic2 * 128:(ic2 + 1) * 128], wnat.r,
                                   wqe, 0, ic2 * 512 + q4 * 128, pbank=2 + (ic2 % 2))
        for tt in range(NT):
            pbk = ctx.pb[tt % 2]

            def emit_mm(e, tt=tt, pbk=pbk):
                inst = None
                for ic2 in range(8):
                    tgt, tv = qanf[ic2]
                    inst = e.matmul(pbk.h[:, :], tv[:, tt * 128:(tt + 1) * 128],
                                    wqe.h[:, ic2 * 512:(ic2 + 1) * 512],
                                    start=(ic2 == 0), stop=(ic2 == 7))
                return inst
            b.op("tensor", emit_mm,
                 reads=tuple([qanf[i][0].r for i in range(8)] + [wqe.r]),
                 writes=(pbk.r,))
            b.op("vector", lambda e, pbk=pbk: e.tensor_copy(qtm.h[:, :], pbk.h[:, :]),
                 reads=(pbk.r,), writes=(qtm.r,))
            b.dma("sync", lambda e, tt=tt: e.dma_start(
                out=cst_q.h[:, :], in_=cs_q_d.a[tt * 128:(tt + 1) * 128, :]),
                reads=(cs_q_d.r,), writes=(cst_q.r,))
            emit_rope_tm(ctx, qtm.h[:, :], qtm.r, eva, 0, cst_q, cst_q.r,
                         L1_SCALE, 4, 128)
            for hh in range(4):
                pview = ctx.pb[2 + (hh % 2)].h[:, :].bitcast(BF16)
                b.op("tensor", lambda e, hh=hh, pview=pview: e.transpose(
                    pview[0:128, 0:128], eva.h[:, hh * 128:(hh + 1) * 128],
                    ctx.ident.h[:, :]),
                    reads=(eva.r, ctx.ident.r), writes=(ctx.pb[2 + (hh % 2)].r,))
                b.op("vector", lambda e, hh=hh, pview=pview: e.tensor_copy(
                    evaf.h[:, :].bitcast(BF16)[:, 0:128], pview[0:128, 0:128]),
                    reads=(ctx.pb[2 + (hh % 2)].r,), writes=(evaf.r,))
                r0 = (oq * 4 + hh) * 128
                b.dma("sync", lambda e, r0=r0, tt=tt: e.dma_start(
                    out=q_d.a[r0:r0 + 128, tt * 128:(tt + 1) * 128],
                    in_=evaf.h[:, :].bitcast(BF16)[:, 0:128]),
                    reads=(evaf.r,), writes=(q_d.r,))

    for p0, pw in ((0, 128), (128, 128), (256, 64)):
        b.dma("gpsimd", lambda e, p0=p0, pw=pw: e.dma_start(
            out=wnat.h[0:pw, :], in_=wkv_a.a[p0:p0 + pw, :]),
            reads=(wkv_a.r,), writes=(wnat.r,))
        for ic in range(NC):
            ctx.transpose_tile(wnat.h[0:pw, ic * 128:(ic + 1) * 128], wnat.r,
                               wvt, 0, ic * 320 + p0, pbank=2 + (ic % 2),
                               np_part=pw, n_cols=128)
    for tt in range(NT):
        pbk = ctx.pb[tt % 2]

        def emit_mm(e, tt=tt, pbk=pbk):
            inst = None
            for ic in range(NC):
                inst = e.matmul(pbk.h[:, 0:320], ctx.xh[ic].h[:, tt * 128:(tt + 1) * 128],
                                wvt.h[:, ic * 320:(ic + 1) * 320],
                                start=(ic == 0), stop=(ic == NC - 1))
            return inst
        b.op("tensor", emit_mm,
             reads=tuple([wvt.r] + [t.r for t in ctx.xh]), writes=(pbk.r,))
        b.op("vector", lambda e, pbk=pbk: e.tensor_copy(ckvt.h[:, :], pbk.h[:, 0:320]),
             reads=(pbk.r,), writes=(ckvt.r,))
        b.op("scalar", lambda e: e.activation(
            ctx.sbf[0].h[:, 0:256], ckvt.h[:, 0:256], AF.Square,
            accum_out=col_b.h[:, 0:1]),
            reads=(ckvt.r,), writes=(ctx.sbf[0].r, col_b.r))
        b.op("vector", lambda e: e.tensor_single_scalar(
            col_b.h[:, :], col_b.h[:, :], 1.0 / 256.0, ALU.mult),
            reads=(col_b.r,), writes=(col_b.r,))
        b.op("scalar", lambda e: e.activation(col_b.h[:, :], col_b.h[:, :], AF.Sqrt,
                                              bias=eps_rms),
             reads=(col_b.r, eps_rms_r), writes=(col_b.r,))
        b.op("vector", lambda e: e.reciprocal(col_b.h[:, :], col_b.h[:, :]),
             reads=(col_b.r,), writes=(col_b.r,))
        b.op("vector", lambda e: e.tensor_single_scalar(
            ctx.sbf[0].h[:, 0:256], ckvt.h[:, 0:256], col_b.h[:, 0:1], ALU.mult),
            reads=(ckvt.r, col_b.r), writes=(ctx.sbf[0].r,))
        b.op("vector", lambda e: e.tensor_mul(
            ctx.sbf[1].h[:, 0:256], ctx.sbf[0].h[:, 0:256], kvlnw_b.h[:, :]),
            reads=(ctx.sbf[0].r, kvlnw_b.r), writes=(ctx.sbf[1].r,))
        for ic in range(2):
            pview = ctx.pb[2 + (ic % 2)].h[:, :].bitcast(BF16)
            b.op("tensor", lambda e, ic=ic, pview=pview: e.transpose(
                pview[0:128, 0:128], ctx.sbf[1].h[:, ic * 128:(ic + 1) * 128],
                ctx.ident.h[:, :]),
                reads=(ctx.sbf[1].r, ctx.ident.r), writes=(ctx.pb[2 + (ic % 2)].r,))
            b.op("vector", lambda e, ic=ic, tt=tt, pview=pview: e.tensor_copy(
                ckvnf.h[:, ic * 1024 + tt * 128:ic * 1024 + (tt + 1) * 128],
                pview[0:128, 0:128]),
                reads=(ctx.pb[2 + (ic % 2)].r,), writes=(ckvnf.r,))
        b.dma("sync", lambda e, tt=tt: e.dma_start(
            out=cst_k.h[:, :], in_=cs_k_d.a[tt * 128:(tt + 1) * 128, :]),
            reads=(cs_k_d.r,), writes=(cst_k.r,))
        emit_rope_tm(ctx, ckvt.h[:, 192:320], ckvt.r, eva, 0, cst_k, cst_k.r,
                     None, 1, 128)
        pview = ctx.pb[2].h[:, :].bitcast(BF16)
        b.op("tensor", lambda e, pview=pview: e.transpose(
            pview[0:64, 0:128], eva.h[:, 64:128], ctx.ident.h[:, :]),
            reads=(eva.r, ctx.ident.r), writes=(ctx.pb[2].r,))
        b.op("vector", lambda e, tt=tt, pview=pview: e.tensor_copy(
            kpef.h[0:64, tt * 128:(tt + 1) * 128], pview[0:64, 0:128]),
            reads=(ctx.pb[2].r,), writes=(kpef.r,))

    for hp in range(16):
        for u in range(2):
            h = 2 * hp + u
            r0 = h * 192
            b.dma("gpsimd", lambda e, r0=r0, u=u: e.dma_start(
                out=wknat.h[u * 64:(u + 1) * 64, :], in_=wkv_b.a[r0:r0 + 64, :]),
                reads=(wkv_b.r,), writes=(wknat.r,))
        for ic in range(2):
            ctx.transpose_tile(wknat.h[:, ic * 128:(ic + 1) * 128], wknat.r,
                               wkbt, 0, ic * 128, pbank=2 + (ic % 2))
        for th in range(2):
            pbk = ctx.pb[th]
            b.op("tensor", lambda e, th=th, pbk=pbk: (
                e.matmul(pbk.h[:, :], wkbt.h[:, 0:128],
                         ckvnf.h[:, th * 512:(th + 1) * 512], start=True, stop=False),
                e.matmul(pbk.h[:, :], wkbt.h[:, 128:256],
                         ckvnf.h[:, 1024 + th * 512:1024 + (th + 1) * 512],
                         start=False, stop=True))[1],
                reads=(wkbt.r, ckvnf.r), writes=(pbk.r,))
            b.op("vector", lambda e, th=th, pbk=pbk: e.tensor_copy(
                eva.h[:, th * 512:(th + 1) * 512], pbk.h[:, :]),
                reads=(pbk.r,), writes=(eva.r,))
        b.dma("sync", lambda e, hp=hp: e.dma_start(
            out=kfm_d.a[hp * 128:(hp + 1) * 128, :], in_=eva.h[:, :]),
            reads=(eva.r,), writes=(kfm_d.r,))
    b.dma("sync", lambda e: e.dma_start(out=kfm_d.a[2048:2112, :], in_=kpef.h[0:64, :]),
          reads=(kpef.r,), writes=(kfm_d.r,))
    for h in range(L1H):
        r0 = h * 192 + 64
        b.dma("gpsimd", lambda e, r0=r0: e.dma_start(
            out=wknat.h[:, :], in_=wkv_b.a[r0:r0 + 128, :]),
            reads=(wkv_b.r,), writes=(wknat.r,))
        for ic in range(2):
            ctx.transpose_tile(wknat.h[:, ic * 128:(ic + 1) * 128], wknat.r,
                               wvt, 0, ic * 4096 + h * 128, pbank=2 + (ic % 2))
    for tt in range(NT):
        for vq in range(8):
            pbk = ctx.pb[vq % 2]
            b.op("tensor", lambda e, tt=tt, vq=vq, pbk=pbk: (
                e.matmul(pbk.h[:, :], ckvnf.h[:, tt * 128:(tt + 1) * 128],
                         wvt.h[:, vq * 512:(vq + 1) * 512], start=True, stop=False),
                e.matmul(pbk.h[:, :], ckvnf.h[:, 1024 + tt * 128:1024 + (tt + 1) * 128],
                         wvt.h[:, 4096 + vq * 512:4096 + (vq + 1) * 512],
                         start=False, stop=True))[1],
                reads=(ckvnf.r, wvt.r), writes=(pbk.r,))
            b.op("vector", lambda e, vq=vq, pbk=pbk: e.tensor_copy(
                eva.h[:, 0:512], pbk.h[:, :]),
                reads=(pbk.r,), writes=(eva.r,))
            b.dma("sync", lambda e, tt=tt, vq=vq: e.dma_start(
                out=v_own.a[tt * 128:(tt + 1) * 128, vq * 512:(vq + 1) * 512],
                in_=eva.h[:, 0:512]),
                reads=(eva.r,), writes=(v_own.r,))

    groups = [[0, 1, 2, 3], [4, 5, 6, 7]]
    b.collective(lambda e: e.collective_compute(
        "AllGather", ALU.bypass, replica_groups=groups,
        ins=[kfm_d.a[:, :].opt()], outs=[kg_d.a[:, :].opt()]),
        reads=(kfm_d.r,), writes=(kg_d.r,))
    b.collective(lambda e: e.collective_compute(
        "AllGather", ALU.bypass, replica_groups=groups,
        ins=[v_own.a[:, :].opt()], outs=[vg_d.a[:, :].opt()]),
        reads=(v_own.r,), writes=(vg_d.r,))

    for h in range(L1H):
        ksrc_n = kg_d.a.rearrange("(r f) t -> f r t", r=4)[h * 64:(h + 1) * 64, :, :]
        b.dma("sync", lambda e, ksrc_n=ksrc_n: e.dma_start(
            out=kh_ap[0:64, :].rearrange("p (r t) -> p r t", r=4), in_=ksrc_n),
            reads=(kg_d.r,), writes=(wvt.r,))
        ksrc_p = kg_d.a.rearrange("(r f) t -> f r t", r=4)[2048:2112, :, :]
        b.dma("sync", lambda e, ksrc_p=ksrc_p: e.dma_start(
            out=kh_ap[64:128, :].rearrange("p (r t) -> p r t", r=4), in_=ksrc_p),
            reads=(kg_d.r,), writes=(wvt.r,))
        vsrc = vg_d.a.rearrange("(g p) v -> p g v", p=128)[:, :, h * 128:(h + 1) * 128]
        b.dma("sync", lambda e, vsrc=vsrc: e.dma_start(
            out=vh_ap.rearrange("p (g v) -> p g v", g=32), in_=vsrc),
            reads=(vg_d.r,), writes=(wvt.r,))
        b.dma("sync", lambda e, h=h: e.dma_start(
            out=qh_ap, in_=q_d.a[h * 128:(h + 1) * 128, :]),
            reads=(q_d.r,), writes=(wqe.r,))
        b.dma("sync", lambda e, h=h: e.dma_start(
            out=kdiag.h[0:64, :], in_=kfm_d.a[h * 64:(h + 1) * 64, :]),
            reads=(kfm_d.r,), writes=(kdiag.r,))
        b.dma("sync", lambda e: e.dma_start(
            out=kdiag.h[64:128, :], in_=kfm_d.a[2048:2112, :]),
            reads=(kfm_d.r,), writes=(kdiag.r,))
        vdsrc = v_own.a.rearrange("(qt p) v -> p qt v", p=128)[:, :, h * 128:(h + 1) * 128]
        b.dma("sync", lambda e, vdsrc=vdsrc: e.dma_start(
            out=vdiag.h[:, :].rearrange("p (qt v) -> p qt v", qt=8), in_=vdsrc),
            reads=(v_own.r,), writes=(vdiag.r,))

        for qt in range(NT):
            b.dma("sync", lambda e, qt=qt: e.dma_start(
                out=cvrow.h[0:1, :], in_=cvals_d.a[qt:qt + 1, :]),
                reads=(cvals_d.r,), writes=(cvrow.r,))
            b.op("tensor", lambda e, qt=qt: e.matmul(
                ctx.pb[7].h[:, 0:128], qh_ap[:, qt * 128:(qt + 1) * 128],
                kdiag.h[:, qt * 128:(qt + 1) * 128], start=True, stop=True),
                reads=(wqe.r, kdiag.r), writes=(ctx.pb[7].r,))
            b.op("vector", lambda e: e.tensor_add(
                ctx.pb[7].h[:, 0:128], ctx.pb[7].h[:, 0:128], tri.h[:, :]),
                reads=(ctx.pb[7].r, tri.r), writes=(ctx.pb[7].r,))
            b.op("vector", lambda e: e.reduce_max(
                maxc.h[:, 8:9], ctx.pb[7].h[:, 0:128], AX.X),
                reads=(ctx.pb[7].r,), writes=(maxc.r,))
            b.op("vector", lambda e: e.tensor_copy(sc_diag.h[:, :], ctx.pb[7].h[:, 0:128]),
                 reads=(ctx.pb[7].r,), writes=(sc_diag.r,))
            def emit_sc(e, qt=qt):
                inst = None
                for kc in range(8):
                    e.matmul(ctx.pb[kc].h[:, :], qh_ap[:, qt * 128:(qt + 1) * 128],
                             kh_ap[:, kc * 512:(kc + 1) * 512], start=True, stop=False)
                    inst = e.matmul(ctx.pb[kc].h[:, :], ctx.oner_bf.h[0:1, 0:128],
                                    cvrow.h[0:1, kc * 512:(kc + 1) * 512],
                                    start=False, stop=True)
                return inst
            b.op("tensor", emit_sc, reads=(wqe.r, wvt.r, cvrow.r, ctx.oner_bf.r),
                 writes=tuple(ctx.pb[i].r for i in range(8)))
            for kc in range(8):
                b.op("vector", lambda e, kc=kc: e.reduce_max(
                    maxc.h[:, kc:kc + 1], ctx.pb[kc].h[:, :], AX.X),
                    reads=(ctx.pb[kc].r,), writes=(maxc.r,))
            b.op("vector", lambda e: e.tensor_reduce(
                negm.h[:, :], maxc.h[:, 0:9], AX.X, ALU.max, negate=True),
                reads=(maxc.r,), writes=(negm.r,))
            for kc in range(8):
                b.op("scalar", lambda e, kc=kc: e.activation(
                    psb_ap[:, kc * 512:(kc + 1) * 512], ctx.pb[kc].h[:, :], AF.Exp,
                    bias=negm.h[:, 0:1], accum_out=sumc.h[:, kc:kc + 1]),
                    reads=(ctx.pb[kc].r, negm.r), writes=(wqe.r, sumc.r))
            b.op("scalar", lambda e: e.activation(
                psb_ap[:, 4096:4224], sc_diag.h[:, :], AF.Exp,
                bias=negm.h[:, 0:1], accum_out=sumc.h[:, 8:9]),
                reads=(sc_diag.r, negm.r), writes=(wqe.r, sumc.r))
            b.op("vector", lambda e, qt=qt: e.tensor_reduce(
                sums8.h[:, qt:qt + 1], sumc.h[:, 0:9], AX.X, ALU.add),
                reads=(sumc.r,), writes=(sums8.r,))
            for g in range(NC):
                ctx.transpose_tile(psb_ap[:, g * 128:(g + 1) * 128], wqe.r,
                                   ctx.xh[g], 0, qt * 128, pbank=2 + (g % 2))
            ctx.transpose_tile(psb_ap[:, 4096:4224], wqe.r, pdiag, 0, qt * 128,
                               pbank=2 + (qt % 2))
        def emit_av(e):
            inst = None
            for g in range(NC):
                for th in range(2):
                    inst = e.matmul(ctx.pb[th].h[:, :],
                                    vh_ap[:, g * 128:(g + 1) * 128],
                                    ctx.xh[g].h[:, th * 512:(th + 1) * 512],
                                    start=(g == 0), stop=False)
            for qt in range(NT):
                th = (qt * 128) // 512
                inst = e.matmul(ctx.pb[th].h[:, (qt * 128) % 512:(qt * 128) % 512 + 128],
                                vdiag.h[:, qt * 128:(qt + 1) * 128],
                                pdiag.h[:, qt * 128:(qt + 1) * 128],
                                start=False, stop=(qt == 3 or qt == 7))
            return inst
        b.op("tensor", emit_av,
             reads=tuple([wvt.r, vdiag.r, pdiag.r] + [t.r for t in ctx.xh]),
             writes=(ctx.pb[0].r, ctx.pb[1].r))
        b.op("vector", lambda e: e.reciprocal(rec8.h[:, :], sums8.h[:, :]),
             reads=(sums8.r,), writes=(rec8.r,))
        b.op("tensor", lambda e: e.transpose(ctx.pb[4].h[0:8, 0:128], rec8.h[:, :],
                                             identf.h[:, :]),
             reads=(rec8.r, identf.r), writes=(ctx.pb[4].r,))
        b.op("vector", lambda e: e.tensor_copy(rrows.h[:, :], ctx.pb[4].h[0:8, 0:128]),
             reads=(ctx.pb[4].r,), writes=(rrows.r,))
        b.dma("sync", lambda e: e.dma_start(out=fl8_d.a[:, :], in_=rrows.h[:, :]),
              reads=(rrows.r,), writes=(fl8_d.r,))
        b.dma("sync", lambda e: e.dma_start(
            out=rrow_ap, in_=fl8_d.a[:, :].rearrange("p c -> (p c)").unsqueeze(0)),
            reads=(fl8_d.r,), writes=(rrow.r,))
        ctx.bcast_row(rrow, ctx.r_b, pbank=5, row_ap=rrow_ap)
        for th in range(2):
            b.op("vector", lambda e, th=th: e.tensor_mul(
                eva.h[:, th * 512:(th + 1) * 512], ctx.pb[th].h[:, :],
                ctx.r_b.h[:, th * 512:(th + 1) * 512]),
                reads=(ctx.pb[th].r, ctx.r_b.r), writes=(eva.r,))
        b.dma("sync", lambda e, h=h: e.dma_start(
            out=ao_d.a[h * 128:(h + 1) * 128, :], in_=eva.h[:, :]),
            reads=(eva.r,), writes=(ao_d.r,))

    for oq in range(8):
        for q4 in range(4):
            oc = 4 * oq + q4
            b.dma("gpsimd", lambda e, oc=oc: e.dma_start(
                out=wnat.h[:, :], in_=wo_d.a[oc * 128:(oc + 1) * 128, :]),
                reads=(wo_d.r,), writes=(wnat.r,))
            for ic in range(NC):
                ctx.transpose_tile(wnat.h[:, ic * 128:(ic + 1) * 128], wnat.r,
                                   wvt, 0, ic * 512 + q4 * 128, pbank=2 + (ic % 2))
        for tt in range(NT):
            b.dma("sync", lambda e, tt=tt: e.dma_start(
                out=aot_ap.rearrange("p (ic t) -> p ic t", ic=32),
                in_=ao_d.a.rearrange("(ic p) t -> p ic t", p=128)[:, :, tt * 128:(tt + 1) * 128]),
                reads=(ao_d.r,), writes=(aot.r,))
            pbk = ctx.pb[tt % 2]

            def emit_mm(e, tt=tt, pbk=pbk):
                inst = None
                for ic in range(NC):
                    inst = e.matmul(pbk.h[:, :], aot_ap[:, ic * 128:(ic + 1) * 128],
                                    wvt.h[:, ic * 512:(ic + 1) * 512],
                                    start=(ic == 0), stop=(ic == NC - 1))
                return inst
            b.op("tensor", emit_mm, reads=(aot.r, wvt.r), writes=(pbk.r,))
            b.op("vector", lambda e, pbk=pbk: e.tensor_copy(
                ctx.xa[0].h[:, 0:512], pbk.h[:, :]),
                reads=(pbk.r,), writes=(ctx.xa[0].r,))
            b.dma("sync", lambda e, tt=tt, oq=oq: e.dma_start(
                out=y_d.a[tt * 128:(tt + 1) * 128, oq * 512:(oq + 1) * 512],
                in_=ctx.xa[0].h[:, 0:512]),
                reads=(ctx.xa[0].r,), writes=(y_d.r,))

    for tt in range(NT):
        b.dma("sync", lambda e, tt=tt: e.dma_start(
            out=ytile_ap, in_=y_d.a[tt * 128:(tt + 1) * 128, :]),
            reads=(y_d.r,), writes=(ytile.r,))
        b.op("vector", lambda e: e.tensor_reduce(
            col_a.h[:, :], ytile_ap, AX.X, ALU.add),
            reads=(ytile.r,), writes=(col_a.r,))
        b.op("vector", lambda e: e.tensor_single_scalar(
            col_a.h[:, :], col_a.h[:, :], 1.0 / 4096.0, ALU.mult),
            reads=(col_a.r,), writes=(col_a.r,))
        b.op("scalar", lambda e: e.activation(
            ctx.sbf[0].h[:, 0:1024], ytile_ap[:, 0:1024], AF.Square,
            accum_out=col_b.h[:, 0:1]),
            reads=(ytile.r,), writes=(ctx.sbf[0].r, col_b.r))
        for j2 in range(1, 4):
            b.op("scalar", lambda e, j2=j2: e.activation(
                ctx.sbf[0].h[:, 0:1024], ytile_ap[:, j2 * 1024:(j2 + 1) * 1024], AF.Square,
                accum_out=ctx.sbf[1].h[:, j2:j2 + 1].bitcast(F32)[:, 0:1] if False
                else maxc.h[:, j2:j2 + 1]),
                reads=(ytile.r,), writes=(ctx.sbf[0].r, maxc.r))
        b.op("vector", lambda e: e.tensor_add(col_b.h[:, :], col_b.h[:, :], maxc.h[:, 1:2]),
             reads=(col_b.r, maxc.r), writes=(col_b.r,))
        b.op("vector", lambda e: e.tensor_add(col_b.h[:, :], col_b.h[:, :], maxc.h[:, 2:3]),
             reads=(col_b.r, maxc.r), writes=(col_b.r,))
        b.op("vector", lambda e: e.tensor_add(col_b.h[:, :], col_b.h[:, :], maxc.h[:, 3:4]),
             reads=(col_b.r, maxc.r), writes=(col_b.r,))
        b.op("vector", lambda e: e.tensor_single_scalar(
            col_b.h[:, :], col_b.h[:, :], 1.0 / 4096.0, ALU.mult),
            reads=(col_b.r,), writes=(col_b.r,))
        b.op("scalar", lambda e: e.activation(maxc.h[:, 0:1], col_a.h[:, :], AF.Square),
             reads=(col_a.r,), writes=(maxc.r,))
        b.op("vector", lambda e: e.tensor_sub(col_b.h[:, :], col_b.h[:, :], maxc.h[:, 0:1]),
             reads=(col_b.r, maxc.r), writes=(col_b.r,))
        eps_ln, eps_ln_r = ctx.cst(EPS_LN)
        b.op("scalar", lambda e: e.activation(col_b.h[:, :], col_b.h[:, :], AF.Sqrt,
                                              bias=eps_ln),
             reads=(col_b.r, eps_ln_r), writes=(col_b.r,))
        b.op("vector", lambda e: e.reciprocal(col_b.h[:, :], col_b.h[:, :]),
             reads=(col_b.r,), writes=(col_b.r,))
        b.op("vector", lambda e: e.tensor_scalar(
            ytile_ap, ytile_ap, col_a.h[:, 0:1], col_b.h[:, 0:1],
            ALU.subtract, ALU.mult),
            reads=(ytile.r, col_a.r, col_b.r), writes=(ytile.r,))
        for fc in range(NC):
            pbk = ctx.pb[2 + (fc % 2)]
            b.op("tensor", lambda e, fc=fc, pbk=pbk: e.transpose(
                pbk.h[:, 0:128], ytile_ap[:, fc * 128:(fc + 1) * 128], identf.h[:, :]),
                reads=(ytile.r, identf.r), writes=(pbk.r,))
            b.op("vector", lambda e, pbk=pbk: e.tensor_copy(evaf.h[:, :], pbk.h[:, 0:128]),
                 reads=(pbk.r,), writes=(evaf.r,))
            b.dma("sync", lambda e, fc=fc, tt=tt: e.dma_start(
                out=lny_d.a[fc * 128:(fc + 1) * 128, tt * 128:(tt + 1) * 128],
                in_=evaf.h[:, :]),
                reads=(evaf.r,), writes=(lny_d.r,))


def resid_add_pass(ctx, add_d, xsrc_d, xdst_d):
    b = ctx.b
    for c in range(NC):
        xa = ctx.xa[c % 2]
        ya = ctx.ya[c % 2]
        b.dma("sync", lambda e, c=c, xa=xa: e.dma_start(
            out=xa.h[:, :], in_=xsrc_d.a[c * 128:(c + 1) * 128, :]),
            reads=(xsrc_d.r,), writes=(xa.r,))
        b.dma("sync", lambda e, c=c, ya=ya: e.dma_start(
            out=ya.h[:, :], in_=add_d.a[c * 128:(c + 1) * 128, :]),
            reads=(add_d.r,), writes=(ya.r,))
        b.op("vector", lambda e, xa=xa, ya=ya: e.tensor_add(xa.h[:, :], xa.h[:, :], ya.h[:, :]),
             reads=(xa.r, ya.r), writes=(xa.r,))
        b.dma("sync", lambda e, c=c, xa=xa: e.dma_start(
            out=xdst_d.a[c * 128:(c + 1) * 128, :], in_=xa.h[:, :]),
            reads=(xa.r,), writes=(xdst_d.r,))



def host_consts():
    return {
        "c_ident": np.eye(128, dtype=BF),
        "c_ones": np.ones((128, 512), np.float32),
    }



def build_l1f_program(standalone=True):
    nc = bass.Bass("TRN2", target_bir_lowering=False, debug=False)
    stack = ExitStack()
    bld = Builder(nc, stack)

    x_in = dram(nc, "x_in", [DM, NTOK], F32, kind="ExternalInput")
    x_out = dram(nc, "x_out", [DM, NTOK], F32, kind="ExternalOutput")
    ident_d = dram(nc, "c_ident", [128, 128], BF16, kind="ExternalInput")
    ones_d = dram(nc, "c_ones", [128, 512], F32, kind="ExternalInput")
    io = {
        "packed": dram(nc, "packed", [4096, 2048], I32, kind="ExternalInput"),
        "bs16": dram(nc, "bs16", [4096, 4096], F32, kind="ExternalInput"),
        "g128": dram(nc, "g128", [128, 1], F32, kind="ExternalInput"),
        "lora_A": dram(nc, "lora_A", [16, 4096], F32, kind="ExternalInput"),
        "lora_B": dram(nc, "lora_B", [4096, 16], F32, kind="ExternalInput"),
        "y_d": dram(nc, "y_d", [DM, NTOK], F32),
    }

    ctx = Ctx(nc, bld)
    ctx.load_consts(ident_d, ones_d)
    ln_stats_of_dram(ctx, x_in)
    xhat_pass(ctx, x_in)
    emit_l1_ffn(ctx, io)
    finalize_ln_stats(ctx, DM, EPS_LN)
    outer_ln_residual(ctx, io["y_d"], x_in, x_out)
    bld.finalize()
    stack.close()
    return nc


def build_l1a_program():
    nc = bass.Bass("TRN2", target_bir_lowering=False, debug=False)
    stack = ExitStack()
    bld = Builder(nc, stack)

    x_in = dram(nc, "x_in", [DM, NTOK], F32, kind="ExternalInput")
    x_out = dram(nc, "x_out", [DM, NTOK], F32, kind="ExternalOutput")
    ident_d = dram(nc, "c_ident", [128, 128], BF16, kind="ExternalInput")
    ones_d = dram(nc, "c_ones", [128, 512], F32, kind="ExternalInput")
    io = {
        "wq_a": dram(nc, "wq_a", [1024, DM], F32, kind="ExternalInput"),
        "wq_b": dram(nc, "wq_b", [DM, 1024], F32, kind="ExternalInput"),
        "wkv_a": dram(nc, "wkv_a", [320, DM], F32, kind="ExternalInput"),
        "wkv_b": dram(nc, "wkv_b", [6144, 256], F32, kind="ExternalInput"),
        "wo": dram(nc, "wo", [DM, DM], F32, kind="ExternalInput"),
        "q_a_ln_w": dram(nc, "q_a_ln_w", [1024], F32, kind="ExternalInput"),
        "kv_a_ln_w": dram(nc, "kv_a_ln_w", [256], F32, kind="ExternalInput"),
        "cs_q": dram(nc, "cs_q", [NTOK, 64], F32, kind="ExternalInput"),
        "cs_k": dram(nc, "cs_k", [NTOK, 64], F32, kind="ExternalInput"),
        "tri": dram(nc, "tri", [128, 128], F32, kind="ExternalInput"),
        "cvals": dram(nc, "cvals", [8, 4096], BF16, kind="ExternalInput"),
        "identf": dram(nc, "c_identf", [128, 128], F32, kind="ExternalInput"),
        "q_d": dram(nc, "q_d", [DM, NTOK], BF16),
        "kfm": dram(nc, "kfm", [2112, NTOK], BF16),
        "v_own": dram(nc, "v_own", [NTOK, DM], BF16),
        "kg": dram(nc, "kg", [4 * 2112, NTOK], BF16),
        "vg": dram(nc, "vg", [SEQ, DM], BF16),
        "ao_d": dram(nc, "ao_d", [DM, NTOK], BF16),
        "y_d": dram(nc, "y_d", [NTOK, DM], F32),
        "lny_d": dram(nc, "lny_d", [DM, NTOK], F32),
        "fl8": dram(nc, "fl8", [8, 128], F32),
    }

    ctx = Ctx(nc, bld)
    ctx.load_consts(ident_d, ones_d)
    ln_stats_of_dram(ctx, x_in)
    xhat_pass(ctx, x_in)
    emit_l1_attn(ctx, io, core_j=None)
    resid_add_pass(ctx, io["lny_d"], x_in, x_out)
    bld.finalize()
    stack.close()
    return nc


def l1a_in_maps(x_full, inputs):
    consts = host_consts()
    cos_h, sin_h = yarn_cos_sin()
    tri = np.where(np.arange(128)[None, :] <= np.arange(128)[:, None],
                   0.0, -1e30).astype(np.float32)
    maps = []
    for c in range(N_CORES):
        s, tiles = core_tiles(c)
        rows = np.concatenate([np.arange(t * 128, (t + 1) * 128) for t in tiles])
        x_fm = np.ascontiguousarray(x_full[s][rows, :].T)
        cs_q = np.concatenate([cos_h[rows] * L1_SCALE, sin_h[rows] * L1_SCALE],
                              axis=1).astype(np.float32)
        cs_k = np.concatenate([cos_h[rows], sin_h[rows]], axis=1).astype(np.float32)
        j = c % 4
        cvals = np.zeros((8, 4096), np.float32)
        for qt in range(8):
            qst = tiles[qt]
            for g in range(32):
                r, i = g // 8, g % 8
                kst = core_tiles(4 * s + r)[1][i]
                cvals[qt, g * 128:(g + 1) * 128] = 0.0 if kst < qst else -1e30
        maps.append({
            "x_in": x_fm, "c_ident": consts["c_ident"], "c_ones": consts["c_ones"],
            "c_identf": np.eye(128, dtype=np.float32),
            "wq_a": inputs["L1_attn_wq_a"], "wq_b": inputs["L1_attn_wq_b"],
            "wkv_a": inputs["L1_attn_wkv_a"], "wkv_b": inputs["L1_attn_wkv_b"],
            "wo": inputs["L1_attn_wo"],
            "q_a_ln_w": inputs["L1_attn_q_a_ln_w"],
            "kv_a_ln_w": inputs["L1_attn_kv_a_ln_w"],
            "cs_q": cs_q, "cs_k": cs_k, "tri": tri,
            "cvals": cvals.astype(BF),
        })
    return maps


def run_l1a(x_full, inputs):
    nc = build_l1a_program()
    maps = l1a_in_maps(x_full, inputs)
    res = run_bass_kernel_spmd(nc, maps, core_ids=list(range(N_CORES)))
    return _gather_fm(res.results)


def build_l0a_program():
    nc = bass.Bass("TRN2", target_bir_lowering=False, debug=False)
    stack = ExitStack()
    bld = Builder(nc, stack)

    x_in = dram(nc, "x_in", [DM, NTOK], F32, kind="ExternalInput")
    x_out = dram(nc, "x_out", [DM, NTOK], F32, kind="ExternalOutput")
    ident_d = dram(nc, "c_ident", [128, 128], BF16, kind="ExternalInput")
    ones_d = dram(nc, "c_ones", [128, 512], F32, kind="ExternalInput")
    io = {
        "wq": dram(nc, "wq", [DM, DM], F32, kind="ExternalInput"),
        "wk": dram(nc, "wk", [DM, DM], F32, kind="ExternalInput"),
        "wv": dram(nc, "wv", [DM, DM], F32, kind="ExternalInput"),
        "bq": dram(nc, "bq", [DM], F32, kind="ExternalInput"),
        "bk": dram(nc, "bk", [DM], F32, kind="ExternalInput"),
        "bv": dram(nc, "bv", [DM], F32, kind="ExternalInput"),
        "bw": dram(nc, "bw", [1, 48], F32, kind="ExternalInput"),
        "ptd": dram(nc, "ptd", [256, 256], F32, kind="ExternalInput"),
        "ptc0": dram(nc, "ptc0", [256, 256], F32, kind="ExternalInput"),
        "ptc1": dram(nc, "ptc1", [256, 256], F32, kind="ExternalInput"),
        "ptc2": dram(nc, "ptc2", [256, 256], F32, kind="ExternalInput"),
        "identf": dram(nc, "c_identf", [128, 128], F32, kind="ExternalInput"),
        "q_d": dram(nc, "q_d", [DM, NTOK], BF16),
        "k_own": dram(nc, "k_own", [DM, NTOK], BF16),
        "v_own": dram(nc, "v_own", [NTOK, DM], BF16),
        "kg": dram(nc, "kg", [4 * DM, NTOK], BF16),
        "vg": dram(nc, "vg", [SEQ, DM], BF16),
        "fl8": dram(nc, "fl8", [8, 128], F32),
        "y_d": dram(nc, "y_d", [DM, NTOK], F32),
    }

    ctx = Ctx(nc, bld)
    ctx.load_consts(ident_d, ones_d)
    ln_stats_of_dram(ctx, x_in)
    xhat_pass(ctx, x_in)
    emit_l0_attn(ctx, io)
    ln_stats_of_dram(ctx, io["y_d"])
    outer_ln_residual(ctx, io["y_d"], x_in, x_out)
    bld.finalize()
    stack.close()
    return nc


def l0a_in_maps(x_full, inputs):
    consts = host_consts()
    ptd, ptc0, ptc1, ptc2 = ga_pt_consts()
    maps = []
    for c in range(N_CORES):
        s, tiles = core_tiles(c)
        rows = np.concatenate([np.arange(t * 128, (t + 1) * 128) for t in tiles])
        x_fm = np.ascontiguousarray(x_full[s][rows, :].T)
        maps.append({
            "x_in": x_fm, "c_ident": consts["c_ident"], "c_ones": consts["c_ones"],
            "c_identf": np.eye(128, dtype=np.float32),
            "wq": inputs["L0_attn_Wq"], "wk": inputs["L0_attn_Wk"], "wv": inputs["L0_attn_Wv"],
            "bq": inputs["L0_attn_bq"], "bk": inputs["L0_attn_bk"], "bv": inputs["L0_attn_bv"],
            "bw": inputs["L0_attn_bias_weight"].reshape(1, 48),
            "ptd": ptd, "ptc0": ptc0, "ptc1": ptc1, "ptc2": ptc2,
        })
    return maps


def run_l0a(x_full, inputs):
    nc = build_l0a_program()
    maps = l0a_in_maps(x_full, inputs)
    res = run_bass_kernel_spmd(nc, maps, core_ids=list(range(N_CORES)))
    return _gather_fm(res.results)


def build_l0f_program():
    nc = bass.Bass("TRN2", target_bir_lowering=False, debug=False)
    stack = ExitStack()
    bld = Builder(nc, stack)

    x_in = dram(nc, "x_in", [DM, NTOK], F32, kind="ExternalInput")
    x_out = dram(nc, "x_out", [DM, NTOK], F32, kind="ExternalOutput")
    ident_d = dram(nc, "c_ident", [128, 128], BF16, kind="ExternalInput")
    ones_d = dram(nc, "c_ones", [128, 512], F32, kind="ExternalInput")
    io = {
        "w_gate": dram(nc, "w_gate", [8 * H8, D8], F32, kind="ExternalInput"),
        "w_up": dram(nc, "w_up", [8 * H8, D8], F32, kind="ExternalInput"),
        "w_down": dram(nc, "w_down", [8 * D8, H8], F32, kind="ExternalInput"),
        "h_d": dram(nc, "h_d", [8 * H8, NTOK], BF16),
        "wtd_d": dram(nc, "wtd_d", [4096, 1408], BF16),
        "y_d": dram(nc, "y_d", [DM, NTOK], F32),
    }

    ctx = Ctx(nc, bld)
    ctx.load_consts(ident_d, ones_d)
    ln_stats_of_dram(ctx, x_in)
    xhat_pass(ctx, x_in)
    emit_l0_ffn(ctx, io)
    finalize_ln_stats(ctx, DM, EPS_LN)
    outer_ln_residual(ctx, io["y_d"], x_in, x_out)
    bld.finalize()
    stack.close()
    return nc


def run_l0f(x_full, inputs):
    nc = build_l0f_program()
    consts = host_consts()
    maps = []
    for c in range(N_CORES):
        s, tiles = core_tiles(c)
        rows = np.concatenate([np.arange(t * 128, (t + 1) * 128) for t in tiles])
        x_fm = np.ascontiguousarray(x_full[s][rows, :].T)
        maps.append({
            "x_in": x_fm,
            "c_ident": consts["c_ident"],
            "c_ones": consts["c_ones"],
            "w_gate": inputs["L0_ffn_w_gate"].reshape(8 * H8, D8),
            "w_up": inputs["L0_ffn_w_up"].reshape(8 * H8, D8),
            "w_down": inputs["L0_ffn_w_down"].reshape(8 * D8, H8),
        })
    res = run_bass_kernel_spmd(nc, maps, core_ids=list(range(N_CORES)))
    return _gather_fm(res.results)



def _slice_inputs_l1f(x_full, inputs):
    consts = host_consts()
    bs16 = np.repeat(inputs["L1_ffn_block_scales"], 16, axis=1).astype(np.float32)
    g128 = np.ascontiguousarray(
        np.broadcast_to(inputs["L1_ffn_global_scale"].reshape(1, 1), (128, 1))).astype(np.float32)
    maps = []
    for c in range(N_CORES):
        s, tiles = core_tiles(c)
        rows = np.concatenate([np.arange(t * 128, (t + 1) * 128) for t in tiles])
        x_fm = np.ascontiguousarray(x_full[s][rows, :].T)
        maps.append({
            "x_in": x_fm,
            "c_ident": consts["c_ident"],
            "c_ones": consts["c_ones"],
            "packed": inputs["L1_ffn_packed_weight"],
            "bs16": bs16,
            "g128": g128,
            "lora_A": inputs["L1_ffn_lora_A"],
            "lora_B": inputs["L1_ffn_lora_B"],
        })
    return maps


def _gather_fm(results, key="x_out"):
    out = np.zeros((2, SEQ, DM), np.float32)
    for c in range(N_CORES):
        s, tiles = core_tiles(c)
        rows = np.concatenate([np.arange(t * 128, (t + 1) * 128) for t in tiles])
        out[s][rows, :] = results[c][key].T
    return out


def run_l1f(x_full, inputs):
    nc = build_l1f_program()
    maps = _slice_inputs_l1f(x_full, inputs)
    res = run_bass_kernel_spmd(nc, maps, core_ids=list(range(N_CORES)))
    return _gather_fm(res.results)


def kernel(**inputs):
    x0 = np.asarray(inputs["L0_attn_x"])
    x1 = run_l0a(x0, inputs)
    x2 = run_l0f(x1, inputs)
    x3 = run_l1a(x2, inputs)
    x4 = run_l1f(x3, inputs)
    return x4.astype(np.float32)
